# version 12
# baseline (speedup 1.0000x reference)
import numpy as np
import ml_dtypes
import concourse.bass as bass
import concourse.mybir as mybir
from concourse.bass_utils import run_bass_kernel_spmd
from contextlib import ExitStack

F32 = mybir.dt.float32
BF16 = mybir.dt.bfloat16
AF = mybir.ActivationFunctionType
ALU = mybir.AluOpType
AX = mybir.AxisListType
BF = ml_dtypes.bfloat16

B, T, DM = 4, 2048, 2048
HQ, HK, HD = 16, 4, 128
NCH = DM // 128
E, IEXP = 16, 1408
NIC = IEXP // 128
WINDOW = 1024
SCALE = 1.0 / float(np.sqrt(128.0))
GP_SLICES = [128] * 11
GP_SLICE_OFF = [128 * i for i in range(11)]

ENGS = ("pe", "act", "dve", "sp", "pl")


def tmax(*toks):
    d = {}
    stack = list(toks)
    while stack:
        t = stack.pop()
        if t is None:
            continue
        if isinstance(t, list):
            stack.extend(t)
            continue
        s, v = t
        if d.get(s, -1) < v:
            d[s] = v
    return list(d.items())


class Guard:

    def __init__(self):
        self.wtok = None
        self.rtoks = []

    def pre_write(self):
        return tmax(self.wtok, *self.rtoks)

    def post_write(self, tok):
        self.wtok = tok
        self.rtoks = []

    def pre_read(self):
        return tmax(self.wtok)

    def post_read(self, tok):
        self.rtoks.append(tok)

    def seed(self, waitlist):
        self.rtoks.extend(list(waitlist))


def scope_toks(*guards):
    return tmax(*[g.wtok for g in guards],
                *[t for g in guards for t in g.rtoks])


class Bank(Guard):
    def __init__(self, handle):
        super().__init__()
        self.t = handle

    def ap(self):
        return self.t.ap()


class Sched:
    def __init__(self):
        self.ops = {e: [] for e in ENGS}
        self.cnt = {}
        self.last_wait = {e: {} for e in ENGS}
        self.eng_sem = {"pe": "pe", "act": "act", "dve": "dve"}

    SELF_WAIT = ("act", "dve")

    def _need(self, eng, sem, val):
        if val is None or val <= 0:
            return False
        if self.eng_sem.get(eng) == sem and eng not in self.SELF_WAIT:
            return False
        if self.last_wait[eng].get(sem, 0) >= val:
            return False
        return True

    def raw(self, eng, fn, waits=(), inc=None, amt=1):
        ws = []
        for (s, v) in waits:
            assert v <= self.cnt.get(s, 0), (
                f"wait on future: {s}>={v} sched={self.cnt.get(s, 0)}")
            if self._need(eng, s, v):
                ws.append((s, v))
                self.last_wait[eng][s] = v
        tok = None
        if inc is not None:
            self.cnt[inc] = self.cnt.get(inc, 0) + amt
            tok = (inc, self.cnt[inc])
        self.ops[eng].append((fn, ws, inc, amt))
        return tok

    def op(self, eng, fn, reads=(), writes=(), waits=(), sem=None, amt=1):
        if sem is None:
            sem = self.eng_sem[eng]
        allw = tmax(list(waits),
                    *[g.pre_read() for g in reads],
                    *[g.pre_write() for g in writes])
        tok = self.raw(eng, fn, waits=allw, inc=sem, amt=amt)
        for g in reads:
            g.post_read(tok)
        for g in writes:
            g.post_write(tok)
        return tok

    def dma(self, queue_eng, sem, out, in_, reads=(), writes=(), waits=()):
        def fn(eng, out=out, in_=in_):
            return eng.dma_start(out=out, in_=in_)
        return self.op(queue_eng, fn, reads=reads, writes=writes, waits=waits,
                       sem=sem, amt=16)

    def group(self, eng, fns, reads=(), writes=(), waits=()):
        allw = tmax(list(waits),
                    *[g.pre_read() for g in reads],
                    *[g.pre_write() for g in writes])
        tok = None
        n = len(fns)
        for i, fn in enumerate(fns):
            tok = self.raw(eng, fn, waits=allw if i == 0 else (),
                           inc=self.eng_sem[eng] if i == n - 1 else None)
        for g in reads:
            g.post_read(tok)
        for g in writes:
            g.post_write(tok)
        return tok

    def pe_now(self):
        return ("pe", self.cnt.get("pe", 0))


def build_program(debug=False):
    nc = bass.Bass("TRN2", target_bir_lowering=False)

    def din(name, shape, dt=F32):
        return nc.dram_tensor(name, list(shape), dt, kind="ExternalInput").ap()

    xcp = din("xcp", (2, 128, NCH * 1024))
    wq_p = din("wq_p", (16, 128, NCH * 128))
    wk_p = din("wk_p", (4, 128, NCH * 128))
    wv_p = din("wv_p", (128, NCH * 512))
    wo_p = din("wo_p", (16, 128, NCH * 128))
    gw_p = din("gw_p", (128, NCH * 16))
    gp_p = din("gp_p", (E, 128, NCH * IEXP))
    up_p = din("up_p", (E, 128, NCH * IEXP))
    dp_p = din("dp_p", (E, 128, 4 * NIC * 512))
    sg_p = din("sg_p", (2, 128, NCH * IEXP))
    su_p = din("su_p", (2, 128, NCH * IEXP))
    sd_p = din("sd_p", (2, 128, 4 * NIC * 512))
    masks_p = din("masks_p", (2, 128, 12 * 512), BF16)
    cosk_p = din("cosk_p", (2, 128, 1024))
    sink_p = din("sink_p", (2, 128, 1024))
    gq_col = din("gq_col", (128, 1))
    gk_col = din("gk_col", (128, 1))
    gq_row = din("gq_row", (1, 128))
    gk_row = din("gk_row", (1, 128))
    biasE_p = din("biasE_p", (128, 128))
    iotaE_p = din("iotaE_p", (128, 128))
    iotaMB_p = din("iotaMB_p", (128, 128))
    ohbig_p = din("ohbig_p", (16, 2048), BF16)
    id128_p = din("id128_p", (128, 128))
    ones_col_p = din("ones_col_p", (128, 1), BF16)
    ones_colf_p = din("ones_colf_p", (128, 1))
    ones_rowf_p = din("ones_rowf_p", (1, 128))

    out_d = nc.dram_tensor("out", [128, NCH * 1024], F32,
                           kind="ExternalOutput").ap()
    dbg = {}
    if debug:
        for nm, shape, dt in [("d_kt", (128, 8192), BF16),
                              ("d_vt", (128, 8192), BF16),
                              ("d_qt", (128, 16384), BF16),
                              ("d_y", (128, 16384), BF16),
                              ("d_h1", (128, 16384), F32),
                              ("d_comb", (128, 128), F32),
                              ("d_scores", (128, 128), F32),
                              ("d_hbf", (128, 16384), BF16),
                              ("d_gw", (128, 256), BF16),
                              ("d_hf", (128, 16384), F32)]:
            dbg[nm] = nc.dram_tensor(nm, list(shape), dt,
                                     kind="ExternalOutput").ap()

    S = Sched()

    with ExitStack() as ctx:
        def sb(name, shape, dt, side=None):
            kw = {"side": side} if side else {}
            return ctx.enter_context(nc.sbuf_tensor(name, list(shape), dt, **kw))

        def psum(name, shape):
            return ctx.enter_context(nc.psum_tensor(name, list(shape), F32))

        R = "right"
        c_gqc = sb("c_gqc", (128, 1), F32, R)
        c_gkc = sb("c_gkc", (128, 1), F32, R)
        c_gqr = sb("c_gqr", (1, 128), F32, R)
        c_gkr = sb("c_gkr", (1, 128), F32, R)
        c_onesc = sb("c_onesc", (128, 1), BF16, R)
        c_onescf = sb("c_onescf", (128, 1), F32, R)
        c_onesrf = sb("c_onesrf", (1, 128), F32, R)
        c_negb = sb("c_negb", (128, 1), F32, R)
        c_rows = sb("c_rows", (1, 1024), F32, R)
        c_rows2 = sb("c_rows2", (1, 1024), F32, R)
        c_rows3 = sb("c_rows3", (1, 1024), F32, R)
        c_rowsB = sb("c_rowsB", (1, 1024), F32, R)
        c_gw = sb("c_gw", (128, NCH * 16), BF16, R)

        pb = [Bank(psum(f"pb{i}", (128, 512))) for i in range(8)]
        G = {n: Guard() for n in
             ("rows", "rows2", "rows3", "rowsB", "rinvb", "combT", "negb")}

        t_cst = None
        for dst, src in [(c_gqc[:], gq_col), (c_gkc[:], gk_col),
                         (c_gqr[:], gq_row), (c_gkr[:], gk_row),
                         (c_onesc[:], ones_col_p),
                         (c_onescf[:], ones_colf_p),
                         (c_onesrf[:], ones_rowf_p)]:
            t_cst = S.dma("sp", "cst", dst, src)

        S.op("dve", lambda e: e.tensor_reduce(
            c_rows[0:1, 0:1], c_gqr[:], axis=AX.X, op=ALU.max,
            apply_absolute_value=True), writes=[G["rows"]], waits=tmax(t_cst))
        S.op("dve", lambda e: e.tensor_reduce(
            c_rows[0:1, 1:2], c_gkr[:], axis=AX.X, op=ALU.max,
            apply_absolute_value=True), writes=[G["rows"]], waits=tmax(t_cst))
        S.op("dve", lambda e: e.tensor_tensor(
            c_rows[0:1, 2:3], c_rows[0:1, 0:1], c_rows[0:1, 1:2], ALU.mult),
            writes=[G["rows"]])
        S.op("dve", lambda e: e.tensor_scalar(
            c_rows[0:1, 3:4], c_rows[0:1, 2:3], -SCALE, None, ALU.mult),
            writes=[G["rows"]])
        S.op("pe", lambda e: e.matmul(
            pb[7].ap()[:, 0:1], c_onesrf[:], c_rows[0:1, 3:4],
            start=True, stop=True), reads=[G["rows"]], writes=[pb[7]],
            waits=tmax(t_cst))
        S.op("dve", lambda e: e.tensor_copy(c_negb[:], pb[7].ap()[:, 0:1]),
             reads=[pb[7]], writes=[G["negb"]])

        def bcast_f32(rowap, rowguard, banks, lhsT=None):
            toks = []
            lh = c_onesrf if lhsT is None else lhsT
            for si, bk in enumerate(banks):
                def f(e, si=si, bk=bk, lh=lh):
                    return e.matmul(bk.ap()[:], lh[:],
                                    rowap[0:1, si * 512:si * 512 + 512],
                                    start=True, stop=True)
                toks.append(S.op("pe", f, reads=[rowguard], writes=[bk],
                                 waits=tmax(t_cst)))
            return toks

        a_big = sb("a_big", (128, 32768), BF16)
        a_Y = sb("a_Y", (128, HQ * 1024), BF16)
        h_f32 = a_big.bitcast(F32)
        h_bf = a_Y
        gY = Guard()
        gHb = gY
        gH = Guard()
        gQT = gH
        gKT = gH
        gVT = gH
        if True:
            a_KT = a_big[:, 0:8192]
            a_VT = a_big[:, 8192:16384]
            a_QT = a_big[:, 16384:32768]

            with ExitStack() as pctx:
                def psb(name, shape, dt):
                    return pctx.enter_context(
                        nc.sbuf_tensor(name, list(shape), dt))

                p_xc = psb("p_xc", (128, NCH * 1024), BF16)
                p_cosk = psb("p_cosk", (128, 1024), F32)
                p_sink = psb("p_sink", (128, 1024), F32)
                p_wt = [psb(f"p_wt{i}", (128, NCH * 128), BF16)
                        for i in range(2)]
                p_wv = psb("p_wv", (128, NCH * 512), BF16)
                p_tmp = [psb(f"p_tmp{i}", (128, 1024), F32)
                         for i in range(2)]
                p_rot = [psb(f"p_rot{i}", (128, 1024), F32)
                         for i in range(2)]
                p_ta = [psb(f"p_ta{i}", (128, 1024), F32) for i in range(2)]
                p_sq = [psb(f"p_sq{i}", (128, 1024), BF16)
                        for i in range(2)]
                gxc, gck, gwt0, gwt1, gwv = (Guard() for _ in range(5))
                gtmp = [Guard(), Guard()]
                grot = [Guard(), Guard()]
                gta = [Guard(), Guard()]
                gsq = [Guard(), Guard()]
                gwt = [gwt0, gwt1]

                t_wv = None

                def proj_stage1(dst_bf, dst_g, dst_off, wt_i, cos_ap,
                                sin_ap, cos_g, g_col):
                    par = wt_i
                    tmp, rot, ta, sq = (p_tmp[par], p_rot[par], p_ta[par],
                                        p_sq[par])
                    gT, gR, gA, gQ = gtmp[par], grot[par], gta[par], gsq[par]
                    for si in range(2):
                        bk = [pb[0], pb[1]][si]
                        fns = []
                        for kc in range(NCH):
                            def f(e, si=si, kc=kc, bk=bk, wt_i=wt_i):
                                return e.matmul(
                                    bk.ap()[:],
                                    p_wt[wt_i][:, kc * 128:(kc + 1) * 128],
                                    p_xc[:, kc * 1024 + si * 512:
                                         kc * 1024 + si * 512 + 512],
                                    start=(kc == 0), stop=(kc == NCH - 1))
                            fns.append(f)
                        S.group("pe", fns, reads=[gwt[wt_i], gxc],
                                writes=[bk])
                    for si in range(2):
                        bk = [pb[0], pb[1]][si]
                        def f(e, si=si, bk=bk, tmp=tmp):
                            return e.copy(tmp[:, si * 512:si * 512 + 512],
                                          bk.ap()[:])
                        S.op("act", f, reads=[bk], writes=[gT])
                        def f2(e, si=si, bk=bk, sq=sq):
                            return e.activation(
                                sq[:, si * 512:si * 512 + 512], bk.ap()[:],
                                AF.Square)
                        S.op("act", f2, reads=[bk], writes=[gQ])
                    S.dma("sp", "shd", rot[0:64, :], tmp[64:128, :],
                          reads=[gT], writes=[gR])
                    S.dma("sp", "shd", rot[64:128, :], tmp[0:64, :],
                          reads=[gT], writes=[gR])
                    S.op("dve", (lambda e, tmp=tmp, ta=ta: e.tensor_tensor(
                        ta[:], tmp[:], cos_ap, ALU.mult)),
                        reads=[gT, cos_g], writes=[gA])
                    S.op("dve", (lambda e, rot=rot: e.tensor_tensor(
                        rot[:], rot[:], sin_ap, ALU.mult)),
                        reads=[cos_g], writes=[gR])
                    S.op("dve", (lambda e, tmp=tmp, ta=ta, rot=rot:
                         e.tensor_tensor(tmp[:], ta[:], rot[:], ALU.add)),
                        reads=[gA, gR], writes=[gT])
                    rowb = [pb[2], pb[3]] if par == 0 else [pb[6], pb[7]]
                    rowg = G["rows"] if par == 0 else G["rowsB"]
                    rows_t = c_rows if par == 0 else c_rowsB
                    for si in range(2):
                        bk = rowb[si]
                        def f(e, si=si, bk=bk, sq=sq):
                            return e.matmul(bk.ap()[0:1, :], c_onesc[:],
                                            sq[:, si * 512:si * 512 + 512],
                                            start=True, stop=True)
                        S.op("pe", f, reads=[gQ], writes=[bk],
                             waits=tmax(t_cst))
                    for si in range(2):
                        bk = rowb[si]
                        def f(e, si=si, bk=bk, rows_t=rows_t):
                            return e.activation(
                                rows_t[0:1, si * 512:si * 512 + 512],
                                bk.ap()[0:1, :], AF.Sqrt)
                        S.op("act", f, reads=[bk], writes=[rowg])
                    S.op("dve", (lambda e, rows_t=rows_t: e.tensor_scalar(
                        rows_t[0:1, :], rows_t[0:1, :], 1e-6, None,
                        ALU.add)), writes=[rowg])
                    S.op("dve", (lambda e, rows_t=rows_t: e.reciprocal(
                        rows_t[0:1, :], rows_t[0:1, :])), writes=[rowg])
                    return (dst_bf, dst_g, dst_off, par, g_col)

                def proj_stage2(st):
                    dst_bf, dst_g, dst_off, par, g_col = st
                    tmp = p_tmp[par]
                    gT = gtmp[par]
                    rowg = G["rows"] if par == 0 else G["rowsB"]
                    rows_t = c_rows if par == 0 else c_rowsB
                    bcb = [pb[4], pb[5]]
                    bcast_f32(rows_t, rowg, bcb, lhsT=g_col)
                    for si in range(2):
                        bk = bcb[si]
                        def f(e, si=si, bk=bk, tmp=tmp):
                            return e.tensor_tensor(
                                dst_bf[:, dst_off + si * 512:
                                       dst_off + si * 512 + 512],
                                tmp[:, si * 512:si * 512 + 512],
                                bk.ap()[:], ALU.mult)
                        S.op("dve", f, reads=[gT, bk], writes=[dst_g])

                pend = []

                def proj_head(dst_bf, dst_g, dst_off, wt_i, cos_ap, sin_ap,
                              cos_g, g_col):
                    st = proj_stage1(dst_bf, dst_g, dst_off, wt_i, cos_ap,
                                     sin_ap, cos_g, g_col)
                    pend.append(st)
                    if len(pend) > 1:
                        proj_stage2(pend.pop(0))

                def proj_flush():
                    while pend:
                        proj_stage2(pend.pop(0))


                for ps in range(2):
                    xw = gxc.pre_write()
                    tok2 = None
                    for half in range(2):
                        def fx(e, ps=ps, half=half):
                            return e.dma_start(
                                out=p_xc[:, half * 8192:(half + 1) * 8192],
                                in_=xcp[ps][:, half * 8192:(half + 1) * 8192])
                        tok2 = S.raw("pl", fx, waits=xw if half == 0 else (),
                                     inc="xc", amt=16)
                    gxc.post_write(tok2)
                    if ps == 0:
                        t_wv = S.dma("pl", "wv", p_wv[:], wv_p,
                                     writes=[gwv])
                        t_gw = S.dma("pl", "gw", c_gw[:], gw_p)
                    S.dma("sp", "ck", p_cosk[:], cosk_p[ps], writes=[gck])
                    S.dma("sp", "ck", p_sink[:], sink_p[ps], writes=[gck])
                    for h in range(HK):
                        S.dma("pl", f"wt{h % 2}", p_wt[h % 2][:], wk_p[h],
                              writes=[gwt[h % 2]])
                        proj_head(a_KT, gKT, h * 2048 + ps * 1024, h % 2,
                                  p_cosk[:], p_sink[:], gck, c_gkr)
                    proj_flush()
                    for j in range(8):
                        sc = ps * 8 + j
                        bk = pb[6] if j % 2 == 0 else pb[7]
                        fns = []
                        for kc in range(NCH):
                            def f(e, kc=kc, j=j, bk=bk):
                                return e.matmul(
                                    bk.ap()[:],
                                    p_xc[:, kc * 1024 + j * 128:
                                         kc * 1024 + j * 128 + 128],
                                    p_wv[:, kc * 512:(kc + 1) * 512],
                                    start=(kc == 0), stop=(kc == NCH - 1))
                            fns.append(f)
                        S.group("pe", fns, reads=[gxc, gwv], writes=[bk])
                        def fv(e, sc=sc, bk=bk):
                            return e.copy(a_VT[:, sc * 512:(sc + 1) * 512],
                                          bk.ap()[:])
                        S.op("act", fv, reads=[bk], writes=[gVT])
                    if ps == 1:
                        for h in range(HQ):
                            S.dma("pl", f"wt{h % 2}", p_wt[h % 2][:],
                                  wq_p[h], writes=[gwt[h % 2]])
                            proj_head(a_QT, gQT, h * 1024, h % 2,
                                      p_cosk[:], p_sink[:], gck, c_gqr)

                proj_flush()
                scope_bar = scope_toks(gxc, gck, gwt0, gwt1, gwv,
                                       *gtmp, *grot, *gta, *gsq)

            with ExitStack() as cctx:
                a_mask = cctx.enter_context(
                    nc.sbuf_tensor("a_mask", [128, 12 * 512], BF16))
                a_PT = cctx.enter_context(
                    nc.sbuf_tensor("a_PT", [128, 12 * 512], BF16))
                c_rinvb = cctx.enter_context(
                    nc.sbuf_tensor("c_rinvb", [128, 512], F32))
                gmask = Guard()
                gPT = [Guard() for _ in range(12)]
                gmask.seed(scope_bar)
                G["rinvb"] = Guard()
                G["rinvb"].seed(scope_bar)
                for _g in gPT:
                    _g.seed(scope_bar)

                for qb in range(2):
                    base = 0 if qb == 0 else 4
                    S.dma("sp", "mk", a_mask[:], masks_p[qb],
                          writes=[gmask])
                    for h in range(HQ):
                        hk = h // 4
                        sbanks = [pb[0], pb[1], pb[6]]
                        for ci in range(12):
                            bk = sbanks[ci % 3]
                            def f(e, ci=ci, bk=bk, h=h, hk=hk, qb=qb,
                                  base=base):
                                return e.matmul(
                                    bk.ap()[:],
                                    a_KT[:, hk * 2048 + (base + ci) * 128:
                                         hk * 2048 + (base + ci) * 128 + 128],
                                    a_QT[:, h * 1024 + qb * 512:
                                         h * 1024 + qb * 512 + 512],
                                    start=True, stop=True)
                            S.op("pe", f, reads=[gKT, gQT], writes=[bk])
                            def fe(e, ci=ci, bk=bk):
                                return e.activation(
                                    a_PT[:, ci * 512:(ci + 1) * 512],
                                    bk.ap()[:], AF.Exp,
                                    bias=c_negb[:, 0:1], scale=SCALE)
                            S.op("act", fe, reads=[bk, G["negb"]],
                                 writes=[gPT[ci]])
                            def fm(e, ci=ci):
                                return e.tensor_tensor(
                                    a_PT[:, ci * 512:(ci + 1) * 512],
                                    a_PT[:, ci * 512:(ci + 1) * 512],
                                    a_mask[:, ci * 512:(ci + 1) * 512],
                                    ALU.mult)
                            S.op("dve", fm, reads=[gmask], writes=[gPT[ci]])
                        fns = []
                        for ci in range(12):
                            def f(e, ci=ci):
                                return e.matmul(
                                    pb[2].ap()[0:1, :], c_onesc[:],
                                    a_PT[:, ci * 512:(ci + 1) * 512],
                                    start=(ci == 0), stop=(ci == 11))
                            fns.append(f)
                        S.group("pe", fns, reads=list(gPT), writes=[pb[2]],
                                waits=tmax(t_cst))
                        fns = []
                        for ci in range(12):
                            def f(e, ci=ci, hk=hk, base=base):
                                scg = base + ci
                                return e.matmul(
                                    pb[4].ap()[:],
                                    a_VT[:, scg * 512 + hk * 128:
                                         scg * 512 + hk * 128 + 128],
                                    a_PT[:, ci * 512:(ci + 1) * 512],
                                    start=(ci == 0), stop=(ci == 11))
                            fns.append(f)
                        S.group("pe", fns, reads=list(gPT) + [gVT],
                                writes=[pb[4]])
                        S.op("dve", lambda e: e.reciprocal(
                            c_rows[0:1, 0:512], pb[2].ap()[0:1, :]),
                            reads=[pb[2]], writes=[G["rows"]])
                        bcast_f32(c_rows, G["rows"], [pb[5]])
                        S.op("dve", lambda e: e.tensor_copy(
                            c_rinvb[:], pb[5].ap()[:]),
                            reads=[pb[5]], writes=[G["rinvb"]])
                        def fr5(e, h=h, qb=qb):
                            return e.tensor_tensor(
                                a_Y[:, h * 1024 + qb * 512:
                                    h * 1024 + qb * 512 + 512],
                                pb[4].ap()[:], c_rinvb[:], ALU.mult)
                        S.op("dve", fr5, reads=[pb[4], G["rinvb"]],
                             writes=[gY])

                scope_bar = tmax(scope_bar,
                                 scope_toks(gmask, G["rinvb"], *gPT))

            bar_cctx = list(scope_bar)
            with ExitStack() as wctx:
                w_wt = [wctx.enter_context(
                    nc.sbuf_tensor(f"w_wt{i}", [128, NCH * 128], BF16,
                                   side="right"))
                    for i in range(2)]
                w_sq = wctx.enter_context(
                    nc.sbuf_tensor("w_sq", [128, 1024], BF16, side="right"))
                w_t1 = wctx.enter_context(
                    nc.sbuf_tensor("w_t1", [128, 1024], F32, side="right"))
                gwo = [Guard(), Guard()]
                gsqb = Guard()
                gt1 = Guard()
                for _g in gwo + [gsqb, gt1]:
                    _g.seed(scope_bar)

                if debug:
                    S.dma("sp", "dbg", dbg["d_kt"], a_big[:, 0:8192],
                          reads=[gH])
                    S.dma("sp", "dbg", dbg["d_vt"], a_big[:, 8192:16384],
                          reads=[gH])
                    S.dma("sp", "dbg", dbg["d_qt"], a_big[:, 16384:32768],
                          reads=[gH])
                    S.dma("sp", "dbg", dbg["d_y"], a_Y[:], reads=[gY])
                xr_w = gH.pre_write()
                def fxr0(e):
                    return e.dma_start(out=h_f32[:, 0:8192],
                                       in_=xcp[1][:, 0:8192])
                t_xr0 = S.raw("sp", fxr0, waits=xr_w, inc="xr", amt=16)
                def fxrw(e):
                    return e.nop()
                S.raw("sp", fxrw, waits=[("xr", 16)])
                def fxr1(e):
                    return e.dma_start(out=h_f32[:, 8192:16384],
                                       in_=xcp[1][:, 8192:16384])
                t_xr1 = S.raw("sp", fxr1, inc="xr", amt=16)
                gH.wtok = t_xr1
                gH.rtoks = []
                xres_half = [t_xr0, t_xr1]

                for m in range(NCH):
                    S.dma("pl", f"wo{m % 2}", w_wt[m % 2][:], wo_p[m],
                          writes=[gwo[m % 2]])
                    banks = [pb[0], pb[1]] if m % 2 == 0 else [pb[6], pb[7]]
                    for si in range(2):
                        bk = banks[si]
                        fns = []
                        for kc in range(NCH):
                            def f(e, m=m, si=si, kc=kc, bk=bk):
                                return e.matmul(
                                    bk.ap()[:],
                                    w_wt[m % 2][:, kc * 128:(kc + 1) * 128],
                                    a_Y[:, kc * 1024 + si * 512:
                                        kc * 1024 + si * 512 + 512],
                                    start=(kc == 0), stop=(kc == NCH - 1))
                            fns.append(f)
                        S.group("pe", fns, reads=[gwo[m % 2], gY],
                                writes=[bk])
                        def fa(e, m=m, si=si, bk=bk):
                            sl = slice(m * 1024 + si * 512,
                                       m * 1024 + si * 512 + 512)
                            return e.tensor_tensor(h_f32[:, sl], h_f32[:, sl],
                                                   bk.ap()[:], ALU.add)
                        tfa = S.raw("dve", fa, waits=tmax(
                            bk.pre_read(), xres_half[m // 8],
                            ("dve", S.cnt.get("dve", 0))), inc="dve")
                        bk.post_read(tfa)
                        gH.wtok = tfa
                    def fs(e, m=m):
                        return e.tensor_tensor(
                            w_sq[:], h_f32[:, m * 1024:(m + 1) * 1024],
                            h_f32[:, m * 1024:(m + 1) * 1024], ALU.mult)
                    S.op("dve", fs, reads=[gH], writes=[gsqb])
                    stats_matmuls(S, pb, c_onescf, c_onesc, h_f32, gH, w_sq,
                                  gsqb, m, tmax(t_cst))
                    wo_done_pe = S.pe_now()
                ln1 = ln_apply(S, nc, pb, G, c_rows, c_rows2, c_rows3,
                               h_f32, gH, h_bf, gHb, w_t1, gt1, bcast_f32)
                if debug:
                    S.dma("sp", "dbg", dbg["d_h1"], h_f32[:], reads=[gH])

            scope_bar = tmax(scope_bar, scope_toks(*gwo, gsqb, gt1))

        with ExitStack() as fctx:
            def fsb(name, shape, dt):
                return fctx.enter_context(
                    nc.sbuf_tensor(name, list(shape), dt))
            f_scores = fsb("f_scores", (128, 8 * 16), F32)
            f_comb = fsb("f_comb", (128, 8 * 16), F32)
            f_sfc = fsb("f_sfc", (128, 128), F32)
            f_gsc = fsb("f_gsc", (128, 32), F32)
            f_gm = fsb("f_gm", (128, 32), F32)
            f_sm = fsb("f_sm", (128, 16), F32)
            f_tmp = fsb("f_tmp", (128, 128), F32)
            f_eq = fsb("f_eq", (128, 128), F32)
            f_sel = fsb("f_sel", (128, 128), F32)
            f_selm = fsb("f_selm", (128, 128), F32)
            f_colA = fsb("f_colA", (128, 8), F32)
            f_colB = fsb("f_colB", (128, 8), F32)
            f_p1 = fsb("f_p1", (128, 32), F32)
            f_gp = [fsb(f"f_gp{i}", (128, NCH * 128), BF16) for i in range(2)]
            f_up = [fsb(f"f_up{i}", (128, NCH * 128), BF16) for i in range(2)]
            f_dp = [fsb(f"f_dp{i}", (128, NIC * 256), BF16)
                    for i in range(2)]
            f_int = fsb("f_int", (128, NIC * 1024), BF16)
            f_t1 = fsb("f_t1", (128, 512), F32)
            f_t2 = fsb("f_t2", (128, 512), F32)
            f_be = fsb("f_be", (128, 1024), F32)
            f_sqt = fsb("f_sqt", (128, 1024), BF16)
            f_out = [fsb(f"f_out{i}", (128, 512), F32) for i in range(2)]
            c_id = fsb("c_id", (128, 128), F32)
            c_biasE = fsb("c_biasE", (128, 128), F32)
            c_iota = fsb("c_iota", (128, 128), F32)
            c_iotaMB = fsb("c_iotaMB", (128, 128), F32)
            c_ohbig = fsb("c_ohbig", (16, 2048), BF16)
            c_combT = fsb("c_combT", (16, 1024), F32)
            c_combTb = fsb("c_combTb", (16, 1024), BF16)
            G["combT"] = Guard()
            G["combT"].seed(scope_bar)
            t_fc = None
            for dst, src in [(c_id[:], id128_p), (c_biasE[:], biasE_p),
                             (c_iota[:], iotaE_p), (c_iotaMB[:], iotaMB_p),
                             (c_ohbig[:], ohbig_p)]:
                t_fc = S.dma("sp", "fcst", dst, src, waits=bar_cctx)
            ggp = [Guard(), Guard()]
            gup = [Guard(), Guard()]
            gdp = [Guard(), Guard()]
            gint = Guard()
            gt1f = Guard()
            gt2f = Guard()
            gbe = Guard()
            gsq2 = Guard()
            gout = [Guard(), Guard()]
            gcomb = Guard()
            gscores = [Guard() for _ in range(8)]
            for _g in (ggp + gup + gdp):
                _g.seed(bar_cctx)
            for _g in (gout + gscores +
                       [gint, gt1f, gt2f, gbe, gsq2, gcomb]):
                _g.seed(scope_bar)

            router(S, nc, pb, ln1, h_bf, gHb, c_gw, t_gw, c_biasE, c_iota,
                   c_iotaMB, f_scores, gscores, f_comb, gcomb, f_sfc, f_gsc,
                   f_gm, f_sm, f_tmp, f_eq, f_sel, f_selm, f_colA, f_colB,
                   f_p1, tmax(t_gw, t_cst, t_fc, scope_bar))

            def expert_gu(g_src, u_src, scaled):
                icg = 0
                for sl, wdt in enumerate(GP_SLICES):
                    nicl = wdt // 128
                    bi = sl % 2
                    coff = NCH * GP_SLICE_OFF[sl]
                    S.dma("pl", f"gp{bi}", f_gp[bi][:, 0:NCH * wdt],
                          g_src[:, coff:coff + NCH * wdt], writes=[ggp[bi]])
                    S.dma("pl", f"up{bi}", f_up[bi][:, 0:NCH * wdt],
                          u_src[:, coff:coff + NCH * wdt], writes=[gup[bi]])
                    for lic in range(nicl):
                        for tb in range(2):
                            par = (icg * 2 + tb) % 2
                            gb = pb[0] if par == 0 else pb[2]
                            ub = pb[1] if par == 0 else pb[3]
                            fns = []
                            for kc in range(NCH):
                                def fg(e, kc=kc, lic=lic, tb=tb, gb=gb,
                                       bi=bi, wdt=wdt):
                                    return e.matmul(
                                        gb.ap()[:],
                                        f_gp[bi][:, kc * wdt + lic * 128:
                                                 kc * wdt + lic * 128 + 128],
                                        h_bf[:, kc * 1024 + tb * 512:
                                             kc * 1024 + tb * 512 + 512],
                                        start=(kc == 0),
                                        stop=(kc == NCH - 1))
                                fns.append(fg)
                            S.group("pe", fns, reads=[ggp[bi], gHb],
                                    writes=[gb])
                            fns = []
                            for kc in range(NCH):
                                def fu(e, kc=kc, lic=lic, tb=tb, ub=ub,
                                       bi=bi, wdt=wdt):
                                    return e.matmul(
                                        ub.ap()[:],
                                        f_up[bi][:, kc * wdt + lic * 128:
                                                 kc * wdt + lic * 128 + 128],
                                        h_bf[:, kc * 1024 + tb * 512:
                                             kc * 1024 + tb * 512 + 512],
                                        start=(kc == 0),
                                        stop=(kc == NCH - 1))
                                fns.append(fu)
                            S.group("pe", fns, reads=[gup[bi], gHb],
                                    writes=[ub])
                            def f1(e, gb=gb):
                                return e.activation(f_t1[:], gb.ap()[:],
                                                    AF.Silu)
                            S.op("act", f1, reads=[gb], writes=[gt1f])
                            dst = f_int[:, icg * 1024 + tb * 512:
                                        icg * 1024 + tb * 512 + 512]
                            if scaled:
                                def f2(e, ub=ub):
                                    return e.tensor_tensor(
                                        f_t2[:], f_t1[:], ub.ap()[:],
                                        ALU.mult)
                                S.op("dve", f2, reads=[gt1f, ub],
                                     writes=[gt2f])
                                def f3(e, dst=dst, tb=tb):
                                    return e.tensor_tensor(
                                        dst, f_t2[:],
                                        f_be[:, tb * 512:tb * 512 + 512],
                                        ALU.mult)
                                S.op("dve", f3, reads=[gt2f, gbe],
                                     writes=[gint])
                            else:
                                def f3(e, ub=ub, dst=dst):
                                    return e.tensor_tensor(
                                        dst, f_t1[:], ub.ap()[:], ALU.mult)
                                S.op("dve", f3, reads=[gt1f, ub],
                                     writes=[gint])
                        icg += 1

            def do_down(dp_src):
                for dq in range(8):
                    di = dq % 2
                    S.dma("pl", f"dp{di}", f_dp[di][:],
                          dp_src[:, dq * NIC * 256:(dq + 1) * NIC * 256],
                          writes=[gdp[di]])
                    for dmi in range(2):
                        m = dq * 2 + dmi
                        for tb in range(2):
                            bk = pb[4] if (dmi * 2 + tb) % 2 == 0 else pb[5]
                            fns = []
                            for kci in range(NIC):
                                def f(e, kci=kci, dmi=dmi, tb=tb, bk=bk,
                                      di=di):
                                    return e.matmul(
                                        bk.ap()[:],
                                        f_dp[di][:, kci * 256 + dmi * 128:
                                             kci * 256 + dmi * 128 + 128],
                                        f_int[:, kci * 1024 + tb * 512:
                                              kci * 1024 + tb * 512 + 512],
                                        start=(kci == 0),
                                        stop=(kci == NIC - 1))
                                fns.append(f)
                            S.group("pe", fns, reads=[gdp[di], gint],
                                    writes=[bk])
                            def fa(e, m=m, tb=tb, bk=bk):
                                sl = slice(m * 1024 + tb * 512,
                                           m * 1024 + tb * 512 + 512)
                                return e.tensor_tensor(
                                    h_f32[:, sl], h_f32[:, sl], bk.ap()[:],
                                    ALU.add)
                            S.op("dve", fa, reads=[bk], writes=[gH])

            expert_gu(sg_p[0], su_p[0], False)
            do_down(sd_p[0])
            for tc in range(8):
                def ftr(e, tc=tc):
                    return e.transpose(pb[1].ap()[0:16, 0:128],
                                       f_comb[:, tc * 16:(tc + 1) * 16],
                                       c_id[:])
                S.op("pe", ftr, reads=[gcomb], writes=[pb[1]],
                     waits=tmax(t_fc))
                def fcp(e, tc=tc):
                    return e.tensor_copy(
                        c_combT[0:16, tc * 128:(tc + 1) * 128],
                        pb[1].ap()[0:16, 0:128])
                S.op("dve", fcp, reads=[pb[1]], writes=[G["combT"]])
            S.op("dve", lambda e: e.tensor_copy(c_combTb[:], c_combT[:]),
                 reads=[G["combT"]], writes=[G["combT"]])
            for eidx in range(E):
                for si in range(2):
                    bk = pb[6 + si]
                    def f(e, si=si, eidx=eidx, bk=bk):
                        return e.matmul(
                            bk.ap()[:],
                            c_ohbig[:, eidx * 128:(eidx + 1) * 128],
                            c_combTb[0:16, si * 512:si * 512 + 512],
                            start=True, stop=True)
                    S.op("pe", f, reads=[G["combT"]], writes=[bk],
                         waits=tmax(t_fc))
                    def fe(e, si=si, bk=bk):
                        return e.tensor_copy(
                            f_be[:, si * 512:si * 512 + 512], bk.ap()[:])
                    S.op("dve", fe, reads=[bk], writes=[gbe])
                expert_gu(gp_p[eidx], up_p[eidx], True)
                do_down(dp_p[eidx])
            expert_gu(sg_p[1], su_p[1], False)
            do_down(sd_p[1])

            if debug:
                S.dma("sp", "dbg", dbg["d_comb"], f_comb[:], reads=[gcomb])
                S.dma("sp", "dbg", dbg["d_scores"], f_scores[:],
                      reads=list(gscores))
                S.dma("sp", "dbg", dbg["d_hbf"], h_bf[:], reads=[gHb])
                S.dma("sp", "dbg", dbg["d_gw"], c_gw[:], waits=tmax(t_gw))
                S.dma("sp", "dbg", dbg["d_hf"], h_f32[:], reads=[gH])
            for m in range(NCH):
                def fs(e, m=m):
                    return e.tensor_tensor(
                        f_sqt[:], h_f32[:, m * 1024:(m + 1) * 1024],
                        h_f32[:, m * 1024:(m + 1) * 1024], ALU.mult)
                S.op("dve", fs, reads=[gH], writes=[gsq2])
                stats_matmuls(S, pb, c_onescf, c_onesc, h_f32, gH, f_sqt,
                              gsq2, m, tmax(t_cst))
            ln_apply(S, nc, pb, G, c_rows, c_rows2, c_rows3, h_f32, gH,
                     None, None, None, None, bcast_f32,
                     out_bufs=f_out, gout=gout, out_dram=out_d)

        S.raw("sp", lambda e: e.nop(),
              waits=[(sm, S.cnt[sm]) for sm in ("od0", "od1", "dbg")
                     if sm in S.cnt])

        sem_handles = {}
        for name in S.cnt.keys():
            sem_handles[name] = ctx.enter_context(nc.semaphore(name))

        with nc.Block() as blk:
            def run(eng_obj, key):
                for fn, ws, inc, amt in S.ops[key]:
                    for (s, v) in ws:
                        eng_obj.wait_ge(sem_handles[s], v)
                    ins = fn(eng_obj)
                    if inc is not None:
                        ins.then_inc(sem_handles[inc], amt)

            @blk.tensor
            def _(e):
                run(e, "pe")

            @blk.scalar
            def _(e):
                run(e, "act")

            @blk.vector
            def _(e):
                run(e, "dve")

            @blk.sync
            def _(e):
                run(e, "sp")

            @blk.gpsimd
            def _(e):
                run(e, "pl")

    return nc


def stats_matmuls(S, pb, onescf, c_onesc, h_src, gHsrc, sqbuf, gsqb, m,
                  cstw):
    for si in range(2):
        def f1(e, m=m, si=si):
            return e.matmul(pb[2 + si].ap()[0:1, :], onescf[:],
                            h_src[:, m * 1024 + si * 512:
                                  m * 1024 + si * 512 + 512],
                            start=(m == 0), stop=(m == NCH - 1))
        if m == 0:
            tok = S.raw("pe", f1, waits=tmax(gHsrc.pre_read(),
                                             pb[2 + si].pre_write(), cstw),
                        inc="pe")
            pb[2 + si].rtoks = []
        else:
            tok = S.raw("pe", f1, waits=tmax(gHsrc.pre_read()),
                        inc="pe")
        gHsrc.post_read(tok)
        if m == NCH - 1:
            pb[2 + si].post_write(tok)
        def f2(e, m=m, si=si):
            return e.matmul(pb[4 + si].ap()[0:1, :], c_onesc[:],
                            sqbuf[:, si * 512:si * 512 + 512],
                            start=(m == 0), stop=(m == NCH - 1))
        if m == 0:
            tok = S.raw("pe", f2, waits=tmax(gsqb.pre_read(),
                                             pb[4 + si].pre_write(), cstw),
                        inc="pe")
            pb[4 + si].rtoks = []
        else:
            tok = S.raw("pe", f2, waits=tmax(gsqb.pre_read()), inc="pe")
        gsqb.post_read(tok)
        if m == NCH - 1:
            pb[4 + si].post_write(tok)


def ln_apply(S, nc, pb, G, c_rows, c_rows2, c_rows3, h_f32, gH, h_bf, gHb,
             tmpbuf, gt1, bcast_f32, out_bufs=None, gout=None, out_dram=None):
    for si in range(2):
        def f(e, si=si):
            return e.tensor_copy(c_rows[0:1, si * 512:si * 512 + 512],
                                 pb[2 + si].ap()[0:1, :])
        S.op("dve", f, reads=[pb[2 + si]], writes=[G["rows"]])
        def f2(e, si=si):
            return e.tensor_copy(c_rows2[0:1, si * 512:si * 512 + 512],
                                 pb[4 + si].ap()[0:1, :])
        S.op("dve", f2, reads=[pb[4 + si]], writes=[G["rows2"]])
    S.op("dve", lambda e: e.tensor_scalar(
        c_rows[0:1, :], c_rows[0:1, :], 1.0 / 2048.0, None, ALU.mult),
        writes=[G["rows"]])
    S.op("dve", lambda e: e.tensor_scalar(
        c_rows2[0:1, :], c_rows2[0:1, :], 1.0 / 2048.0, None, ALU.mult),
        writes=[G["rows2"]])
    S.op("dve", lambda e: e.tensor_tensor(
        c_rows3[0:1, :], c_rows[0:1, :], c_rows[0:1, :], ALU.mult),
        reads=[G["rows"]], writes=[G["rows3"]])
    S.op("dve", lambda e: e.tensor_tensor(
        c_rows2[0:1, :], c_rows2[0:1, :], c_rows3[0:1, :], ALU.subtract),
        reads=[G["rows3"]], writes=[G["rows2"]])
    S.op("dve", lambda e: e.tensor_scalar(
        c_rows2[0:1, :], c_rows2[0:1, :], 1e-5, None, ALU.add),
        writes=[G["rows2"]])
    S.op("act", lambda e: e.activation(
        c_rows3[0:1, :], c_rows2[0:1, :], AF.Sqrt),
        reads=[G["rows2"]], writes=[G["rows3"]])
    S.op("dve", lambda e: e.reciprocal(c_rows2[0:1, :], c_rows3[0:1, :]),
         reads=[G["rows3"]], writes=[G["rows2"]])
    S.op("dve", lambda e: e.tensor_tensor(
        c_rows3[0:1, :], c_rows[0:1, :], c_rows2[0:1, :], ALU.mult),
        reads=[G["rows"], G["rows2"]], writes=[G["rows3"]])
    rb = bcast_f32(c_rows2, G["rows2"], [pb[2], pb[3]])
    mb = bcast_f32(c_rows3, G["rows3"], [pb[4], pb[5]])
    last = None
    for m in range(NCH):
        otoks = []
        for si in range(2):
            rbk = pb[2 + si]
            mbk = pb[4 + si]
            if out_dram is None:
                tb = tmpbuf[:, 0:512]
                tbg = gt1
                dst = h_f32[:, m * 1024 + si * 512:m * 1024 + si * 512 + 512]
                dstg = gH
            else:
                bi = (m * 2 + si) % 2
                tb = out_bufs[bi][:]
                tbg = gout[bi]
                dst = tb
                dstg = gout[bi]
            def f1(e, m=m, si=si, rbk=rbk, tb=tb):
                return e.tensor_tensor(
                    tb,
                    h_f32[:, m * 1024 + si * 512:m * 1024 + si * 512 + 512],
                    rbk.ap()[:], ALU.mult)
            S.op("dve", f1, reads=[rbk, gH], writes=[tbg])
            def f2(e, dst=dst, tb=tb, mbk=mbk):
                return e.tensor_tensor(dst, tb, mbk.ap()[:], ALU.subtract)
            t2 = S.op("dve", f2, reads=[mbk, tbg], writes=[dstg])
            otoks.append(t2)
            if out_dram is not None:
                bi = (m * 2 + si) % 2
                def fst(e, m=m, si=si, bi=bi):
                    return e.dma_start(
                        out=out_dram[:, m * 1024 + si * 512:
                                     m * 1024 + si * 512 + 512],
                        in_=out_bufs[bi][:])
                last = S.op("sp", fst, reads=[gout[bi]], sem=f"od{bi}",
                            amt=16)
        if out_dram is None:
            def f3(e, m=m):
                return e.copy(h_bf[:, m * 1024:(m + 1) * 1024],
                              h_f32[:, m * 1024:(m + 1) * 1024])
            last = S.op("act", f3, reads=[gH], writes=[gHb])
    return last


def router(S, nc, pb, ln1, h_bf, gHb, c_gw, t_gw, c_biasE, c_iota,
           c_iotaMB, f_scores, gscores, f_comb, gcomb, f_sfc, f_gsc, f_gm,
           f_sm, f_tmp, f_eq, f_sel, f_selm, f_colA, f_colB, f_p1, cstw):
    PAIRS4 = [(0, 1), (0, 2), (0, 3), (1, 2), (1, 3), (2, 3)]
    grt = Guard()
    bk = pb[0]
    for tc in range(8):
        fns = []
        for kc in range(NCH):
            def f(e, kc=kc, tc=tc, bk=bk):
                return e.matmul(
                    bk.ap()[:, tc * 16:(tc + 1) * 16],
                    h_bf[:, kc * 1024 + tc * 128:kc * 1024 + tc * 128 + 128],
                    c_gw[:, kc * 16:(kc + 1) * 16],
                    start=(kc == 0), stop=(kc == NCH - 1))
            fns.append(f)
        S.group("pe", fns, reads=[gHb], writes=[bk], waits=cstw)
    S.op("act", lambda e: e.activation(
        f_scores[:], bk.ap()[:, 0:128], AF.Sigmoid),
        reads=[bk], writes=[gscores[0]])

    def seg(ap, i):
        return ap.rearrange("p (s i) -> p s i", i=i)

    def b0(tensor, nseg, i):
        return bass.AP(tensor.ap().tensor, 0, [[nseg, 128], [1, nseg], [0, i]])

    D = lambda fn, **kw: S.op("dve", fn, writes=[grt], **kw)
    D(lambda e: e.tensor_tensor(f_sfc[:], f_scores[:], c_biasE[:], ALU.add),
      reads=[gscores[0]], waits=cstw)
    first = True
    for (i, j) in PAIRS4:
        D(lambda e, i=i, j=j: e.tensor_tensor(
            f_p1[:], f_sfc[:, i:128:4], f_sfc[:, j:128:4], ALU.add))
        if first:
            D(lambda e: e.tensor_copy(f_gsc[:], f_p1[:]))
            first = False
        else:
            D(lambda e: e.tensor_tensor(f_gsc[:], f_gsc[:], f_p1[:],
                                        ALU.max))
    first = True
    for (i, j) in PAIRS4:
        D(lambda e, i=i, j=j: e.tensor_tensor(
            f_colB[:], f_gsc[:, i:32:4], f_gsc[:, j:32:4], ALU.min))
        if first:
            D(lambda e: e.tensor_copy(f_colA[:], f_colB[:]))
            first = False
        else:
            D(lambda e: e.tensor_tensor(f_colA[:], f_colA[:], f_colB[:],
                                        ALU.max))
    D(lambda e: e.tensor_tensor(seg(f_gm[:], 4), seg(f_gsc[:], 4),
                                b0(f_colA, 8, 4), ALU.is_ge))
    D(lambda e: e.tensor_tensor(seg(f_tmp[:], 4), seg(f_sfc[:], 4),
                                b0(f_gm, 32, 4), ALU.mult))
    D(lambda e: e.memset(f_selm[:], 0.0))
    for it in range(6):
        D(lambda e: e.tensor_reduce(f_colA[:], seg(f_tmp[:], 16), axis=AX.X,
                                    op=ALU.max))
        D(lambda e: e.tensor_tensor(seg(f_eq[:], 16), seg(f_tmp[:], 16),
                                    b0(f_colA, 8, 16), ALU.is_equal))
        D(lambda e: e.tensor_tensor(f_sel[:], f_eq[:], c_iotaMB[:],
                                    ALU.mult))
        D(lambda e: e.tensor_scalar(f_sel[:], f_sel[:], 1.0e4, None,
                                    ALU.add))
        D(lambda e: e.tensor_reduce(f_colA[:], seg(f_sel[:], 16), axis=AX.X,
                                    op=ALU.min))
        D(lambda e: e.tensor_tensor(seg(f_sel[:], 16), c_iota[:].rearrange(
            "p (s i) -> p s i", i=16), b0(f_colA, 8, 16), ALU.is_equal))
        D(lambda e: e.tensor_tensor(f_selm[:], f_selm[:], f_sel[:], ALU.add))
        D(lambda e: e.tensor_scalar(f_sel[:], f_sel[:], 1.0e4, None,
                                    ALU.mult))
        D(lambda e: e.tensor_tensor(f_tmp[:], f_tmp[:], f_sel[:],
                                    ALU.subtract))
    D(lambda e: e.tensor_tensor(f_eq[:], f_scores[:], f_selm[:], ALU.mult),
      reads=[gscores[0]])
    D(lambda e: e.tensor_reduce(f_colA[:], seg(f_eq[:], 16), axis=AX.X,
                                op=ALU.add))
    D(lambda e: e.tensor_scalar(f_colA[:], f_colA[:], 1.0e-20, None,
                                ALU.add))
    D(lambda e: e.reciprocal(f_colA[:], f_colA[:]))
    D(lambda e: e.tensor_scalar(f_colA[:], f_colA[:], 2.5, None, ALU.mult))
    S.op("dve", lambda e: e.tensor_tensor(
        seg(f_comb[:], 16), seg(f_eq[:], 16), b0(f_colA, 8, 16), ALU.mult),
        writes=[gcomb, grt])



def _pack_mblocks(wT, nb):
    a = wT.reshape(NCH, 128, nb, 128).transpose(2, 1, 0, 3)
    return np.ascontiguousarray(a.reshape(nb, 128, NCH * 128))


def _pack_rhs(wT):
    N = wT.shape[1]
    a = wT.reshape(NCH, 128, N).transpose(1, 0, 2)
    return np.ascontiguousarray(a.reshape(128, NCH * N))


def _pack_gu_slices(AT):
    ne, Hh, Ii = AT.shape
    A = AT.reshape(ne, NCH, 128, Ii)
    parts = []
    for w, io in zip(GP_SLICES, GP_SLICE_OFF):
        parts.append(A[:, :, :, io:io + w].transpose(0, 2, 1, 3)
                     .reshape(ne, 128, NCH * w))
    return np.ascontiguousarray(np.concatenate(parts, axis=2))


def _pack_dp(AT):
    ne = AT.shape[0]
    A = AT.reshape(ne, NIC, 128, DM)
    parts = []
    for h in range(8):
        parts.append(A[:, :, :, h * 256:(h + 1) * 256].transpose(0, 2, 1, 3)
                     .reshape(ne, 128, NIC * 256))
    return np.ascontiguousarray(np.concatenate(parts, axis=2))


_PROG_CACHE = {}


def _get_prog():
    if "p" not in _PROG_CACHE:
        _PROG_CACHE["p"] = build_program()
    return _PROG_CACHE["p"]


def kernel(L0_attn_x, L0_attn_wq, L0_attn_wk, L0_attn_wv, L0_attn_wo,
           L0_attn_gamma_q, L0_attn_gamma_k, L0_ffn_gate_weight,
           L0_ffn_e_score_correction_bias, L0_ffn_gate_proj, L0_ffn_up_proj,
           L0_ffn_down_proj, L0_ffn_shared_gate_w, L0_ffn_shared_up_w,
           L0_ffn_shared_down_w):
    x = np.asarray(L0_attn_x, dtype=np.float32)
    nc = _get_prog()

    wq_p = _pack_mblocks(np.ascontiguousarray(
        np.asarray(L0_attn_wq, np.float32).T), 16)
    wk_p = _pack_mblocks(np.ascontiguousarray(
        np.asarray(L0_attn_wk, np.float32).T), 4)
    wv_p = _pack_rhs(np.ascontiguousarray(
        np.asarray(L0_attn_wv, np.float32).T))
    wo_p = _pack_mblocks(np.ascontiguousarray(
        np.asarray(L0_attn_wo, np.float32).T), 16)
    gw_p = _pack_rhs(np.ascontiguousarray(
        np.asarray(L0_ffn_gate_weight, np.float32).T))
    gp_p = _pack_gu_slices(np.asarray(L0_ffn_gate_proj, np.float32)
                           .transpose(0, 2, 1))
    up_p = _pack_gu_slices(np.asarray(L0_ffn_up_proj, np.float32)
                           .transpose(0, 2, 1))
    dp_p = _pack_dp(np.asarray(L0_ffn_down_proj, np.float32)
                    .transpose(0, 2, 1))
    sgT = np.asarray(L0_ffn_shared_gate_w, np.float32).T
    suT = np.asarray(L0_ffn_shared_up_w, np.float32).T
    sdT = np.asarray(L0_ffn_shared_down_w, np.float32).T
    sg_p = _pack_gu_slices(np.stack([sgT[:, :IEXP], sgT[:, IEXP:]], axis=0))
    su_p = _pack_gu_slices(np.stack([suT[:, :IEXP], suT[:, IEXP:]], axis=0))
    sd_p = _pack_dp(np.stack([sdT[:IEXP], sdT[IEXP:]], axis=0))

    inv = (1.0 / 10000.0 ** (np.arange(0, HD, 2, dtype=np.float64) / HD))
    tpos = np.arange(T, dtype=np.float64)
    ang = tpos[:, None] * inv[None, :]
    cosT = np.cos(ang).astype(np.float32)
    sinT = np.sin(ang).astype(np.float32)
    cos_full = np.ascontiguousarray(np.concatenate([cosT, cosT], axis=1).T)
    sin_full = np.ascontiguousarray(np.concatenate([sinT, sinT], axis=1).T)
    sign = np.where(np.arange(128) < 64, -1.0, 1.0).astype(np.float32)
    sinS_full = np.ascontiguousarray(sin_full * sign[:, None])

    ones_col = np.ones((128, 1), dtype=BF)
    ones_rowf = np.ones((1, 128), dtype=np.float32)
    id128 = np.eye(128, dtype=np.float32)
    biasE = np.ascontiguousarray(np.broadcast_to(np.tile(
        np.asarray(L0_ffn_e_score_correction_bias, np.float32), 8),
        (128, 128)))
    iotaE = np.ascontiguousarray(np.broadcast_to(
        np.tile(np.arange(16, dtype=np.float32), 8), (128, 128)))
    iotaMB = np.ascontiguousarray(iotaE - 1.0e4)
    ohbig = np.zeros((16, 2048), dtype=BF)
    for e in range(16):
        ohbig[e, e * 128:(e + 1) * 128] = 1.0

    in_maps = []
    for c in range(8):
        b, r0 = c // 2, (c % 2) * 1024
        other = 1024 - r0
        xT = np.ascontiguousarray(x[b].T)
        xr = xT.reshape(NCH, 128, T)
        xcp = np.ascontiguousarray(np.stack([
            xr[:, :, other:other + 1024].transpose(1, 0, 2)
            .reshape(128, NCH * 1024),
            xr[:, :, r0:r0 + 1024].transpose(1, 0, 2)
            .reshape(128, NCH * 1024)], axis=0))
        cosk = np.ascontiguousarray(np.stack(
            [cos_full[:, other:other + 1024], cos_full[:, r0:r0 + 1024]],
            axis=0))
        sink = np.ascontiguousarray(np.stack(
            [sinS_full[:, other:other + 1024], sinS_full[:, r0:r0 + 1024]],
            axis=0))
        pass_base = [other, r0]
        m = np.zeros((2, 128, 12 * 512), dtype=np.float32)
        parts = np.arange(128)
        for qb in range(2):
            for ci in range(12):
                scg = (0 if qb == 0 else 4) + ci
                pidx = scg // 8
                s_abs = pass_base[pidx] + (scg % 8) * 128 + parts
                q_abs = r0 + qb * 512 + np.arange(512)
                dd = q_abs[None, :] - s_abs[:, None]
                m[qb, :, ci * 512:(ci + 1) * 512] = (
                    (dd >= 0) & (dd <= WINDOW)).astype(np.float32)
        masks = m.astype(BF)

        im = {
            "xcp": xcp, "wq_p": wq_p, "wk_p": wk_p, "wv_p": wv_p,
            "wo_p": wo_p, "gw_p": gw_p, "gp_p": gp_p, "up_p": up_p,
            "dp_p": dp_p, "sg_p": sg_p, "su_p": su_p, "sd_p": sd_p,
            "masks_p": masks,
            "cosk_p": cosk, "sink_p": sink,
            "gq_col": np.ascontiguousarray(
                np.asarray(L0_attn_gamma_q, np.float32).reshape(128, 1)),
            "gk_col": np.ascontiguousarray(
                np.asarray(L0_attn_gamma_k, np.float32).reshape(128, 1)),
            "gq_row": np.ascontiguousarray(
                np.asarray(L0_attn_gamma_q, np.float32).reshape(1, 128)),
            "gk_row": np.ascontiguousarray(
                np.asarray(L0_attn_gamma_k, np.float32).reshape(1, 128)),
            "biasE_p": biasE, "iotaE_p": iotaE, "iotaMB_p": iotaMB,
            "ohbig_p": ohbig, "id128_p": id128,
            "ones_col_p": ones_col, "ones_rowf_p": ones_rowf,
            "ones_colf_p": np.ones((128, 1), dtype=np.float32),
        }
        in_maps.append(im)

    res = run_bass_kernel_spmd(nc, in_maps, core_ids=list(range(8)))
    out = np.empty((B, T, DM), dtype=np.float32)
    for c in range(8):
        b, r0 = c // 2, (c % 2) * 1024
        O = np.asarray(res.results[c]["out"])
        out[b, r0:r0 + 1024, :] = (O.reshape(128, NCH, 1024)
                                   .transpose(1, 0, 2).reshape(DM, 1024).T)
    return out


# version 13
# speedup vs baseline: 1.1927x; 1.1927x over previous
import numpy as np
import ml_dtypes
import concourse.bass as bass
import concourse.mybir as mybir
from concourse.bass_utils import run_bass_kernel_spmd
from contextlib import ExitStack

F32 = mybir.dt.float32
BF16 = mybir.dt.bfloat16
AF = mybir.ActivationFunctionType
ALU = mybir.AluOpType
AX = mybir.AxisListType
BF = ml_dtypes.bfloat16

B, T, DM = 4, 2048, 2048
HQ, HK, HD = 16, 4, 128
NCH = DM // 128
E, IEXP = 16, 1408
NIC = IEXP // 128
WINDOW = 1024
SCALE = 1.0 / float(np.sqrt(128.0))
GP_SLICES = [128] * 11
GP_SLICE_OFF = [128 * i for i in range(11)]

ENGS = ("pe", "act", "dve", "sp", "pl")


def tmax(*toks):
    d = {}
    stack = list(toks)
    while stack:
        t = stack.pop()
        if t is None:
            continue
        if isinstance(t, list):
            stack.extend(t)
            continue
        s, v = t
        if d.get(s, -1) < v:
            d[s] = v
    return list(d.items())


class Guard:

    def __init__(self):
        self.wtok = None
        self.rtoks = []

    def pre_write(self):
        return tmax(self.wtok, *self.rtoks)

    def post_write(self, tok):
        self.wtok = tok
        self.rtoks = []

    def pre_read(self):
        return tmax(self.wtok)

    def post_read(self, tok):
        self.rtoks.append(tok)

    def seed(self, waitlist):
        self.rtoks.extend(list(waitlist))


def scope_toks(*guards):
    return tmax(*[g.wtok for g in guards],
                *[t for g in guards for t in g.rtoks])


class Bank(Guard):
    def __init__(self, handle):
        super().__init__()
        self.t = handle

    def ap(self):
        return self.t.ap()


class Sched:
    def __init__(self):
        self.ops = {e: [] for e in ENGS}
        self.cnt = {}
        self.last_wait = {e: {} for e in ENGS}
        self.eng_sem = {"pe": "pe", "act": "act", "dve": "dve"}

    SELF_WAIT = ("act", "dve")

    def _need(self, eng, sem, val):
        if val is None or val <= 0:
            return False
        if self.eng_sem.get(eng) == sem and eng not in self.SELF_WAIT:
            return False
        if self.last_wait[eng].get(sem, 0) >= val:
            return False
        return True

    def raw(self, eng, fn, waits=(), inc=None, amt=1):
        ws = []
        for (s, v) in waits:
            assert v <= self.cnt.get(s, 0), (
                f"wait on future: {s}>={v} sched={self.cnt.get(s, 0)}")
            if self._need(eng, s, v):
                ws.append((s, v))
                self.last_wait[eng][s] = v
        tok = None
        if inc is not None:
            self.cnt[inc] = self.cnt.get(inc, 0) + amt
            tok = (inc, self.cnt[inc])
        self.ops[eng].append((fn, ws, inc, amt))
        return tok

    def op(self, eng, fn, reads=(), writes=(), waits=(), sem=None, amt=1):
        if sem is None:
            sem = self.eng_sem[eng]
        allw = tmax(list(waits),
                    *[g.pre_read() for g in reads],
                    *[g.pre_write() for g in writes])
        tok = self.raw(eng, fn, waits=allw, inc=sem, amt=amt)
        for g in reads:
            g.post_read(tok)
        for g in writes:
            g.post_write(tok)
        return tok

    def dma(self, queue_eng, sem, out, in_, reads=(), writes=(), waits=()):
        def fn(eng, out=out, in_=in_):
            return eng.dma_start(out=out, in_=in_)
        return self.op(queue_eng, fn, reads=reads, writes=writes, waits=waits,
                       sem=sem, amt=16)

    def group(self, eng, fns, reads=(), writes=(), waits=()):
        allw = tmax(list(waits),
                    *[g.pre_read() for g in reads],
                    *[g.pre_write() for g in writes])
        tok = None
        n = len(fns)
        for i, fn in enumerate(fns):
            tok = self.raw(eng, fn, waits=allw if i == 0 else (),
                           inc=self.eng_sem[eng] if i == n - 1 else None)
        for g in reads:
            g.post_read(tok)
        for g in writes:
            g.post_write(tok)
        return tok

    def pe_now(self):
        return ("pe", self.cnt.get("pe", 0))


def build_program(debug=False):
    nc = bass.Bass("TRN2", target_bir_lowering=False)

    def din(name, shape, dt=F32):
        return nc.dram_tensor(name, list(shape), dt, kind="ExternalInput").ap()

    xcp = din("xcp", (2, 128, NCH * 1024))
    wq_p = din("wq_p", (16, 128, NCH * 128))
    wk_p = din("wk_p", (4, 128, NCH * 128))
    wv_p = din("wv_p", (128, NCH * 512))
    wo_p = din("wo_p", (16, 128, NCH * 128))
    gw_p = din("gw_p", (128, NCH * 16))
    gp_p = din("gp_p", (E, 128, NCH * IEXP))
    up_p = din("up_p", (E, 128, NCH * IEXP))
    dp_p = din("dp_p", (E, 128, 4 * NIC * 512))
    sg_p = din("sg_p", (2, 128, NCH * IEXP))
    su_p = din("su_p", (2, 128, NCH * IEXP))
    sd_p = din("sd_p", (2, 128, 4 * NIC * 512))
    masks_p = din("masks_p", (2, 128, 12 * 512), BF16)
    cosk_p = din("cosk_p", (2, 128, 1024))
    sink_p = din("sink_p", (2, 128, 1024))
    gq_col = din("gq_col", (128, 1))
    gk_col = din("gk_col", (128, 1))
    gq_row = din("gq_row", (1, 128))
    gk_row = din("gk_row", (1, 128))
    biasE_p = din("biasE_p", (128, 128))
    iotaE_p = din("iotaE_p", (128, 128))
    iotaMB_p = din("iotaMB_p", (128, 128))
    ohbig_p = din("ohbig_p", (16, 2048), BF16)
    id128_p = din("id128_p", (128, 128))
    ones_col_p = din("ones_col_p", (128, 1), BF16)
    ones_colf_p = din("ones_colf_p", (128, 1))
    ones_rowf_p = din("ones_rowf_p", (1, 128))

    out_d = nc.dram_tensor("out", [128, NCH * 1024], F32,
                           kind="ExternalOutput").ap()
    dbg = {}
    if debug:
        for nm, shape, dt in [("d_kt", (128, 8192), BF16),
                              ("d_vt", (128, 8192), BF16),
                              ("d_qt", (128, 16384), BF16),
                              ("d_y", (128, 16384), BF16),
                              ("d_h1", (128, 16384), F32),
                              ("d_comb", (128, 128), F32),
                              ("d_scores", (128, 128), F32),
                              ("d_hbf", (128, 16384), BF16),
                              ("d_gw", (128, 256), BF16),
                              ("d_hf", (128, 16384), F32)]:
            dbg[nm] = nc.dram_tensor(nm, list(shape), dt,
                                     kind="ExternalOutput").ap()

    S = Sched()

    with ExitStack() as ctx:
        def sb(name, shape, dt, side=None):
            kw = {"side": side} if side else {}
            return ctx.enter_context(nc.sbuf_tensor(name, list(shape), dt, **kw))

        def psum(name, shape):
            return ctx.enter_context(nc.psum_tensor(name, list(shape), F32))

        R = "right"
        c_gqc = sb("c_gqc", (128, 1), F32, R)
        c_gkc = sb("c_gkc", (128, 1), F32, R)
        c_gqr = sb("c_gqr", (1, 128), F32, R)
        c_gkr = sb("c_gkr", (1, 128), F32, R)
        c_onesc = sb("c_onesc", (128, 1), BF16, R)
        c_onescf = sb("c_onescf", (128, 1), F32, R)
        c_onesrf = sb("c_onesrf", (1, 128), F32, R)
        c_negb = sb("c_negb", (128, 1), F32, R)
        c_rows = sb("c_rows", (1, 1024), F32, R)
        c_rows2 = sb("c_rows2", (1, 1024), F32, R)
        c_rows3 = sb("c_rows3", (1, 1024), F32, R)
        c_rowsB = sb("c_rowsB", (1, 1024), F32, R)
        c_gw = sb("c_gw", (128, NCH * 16), BF16, R)

        pb = [Bank(psum(f"pb{i}", (128, 512))) for i in range(8)]
        G = {n: Guard() for n in
             ("rows", "rows2", "rows3", "rowsB", "rinvb", "combT", "negb")}

        t_cst = None
        for dst, src in [(c_gqc[:], gq_col), (c_gkc[:], gk_col),
                         (c_gqr[:], gq_row), (c_gkr[:], gk_row),
                         (c_onesc[:], ones_col_p),
                         (c_onescf[:], ones_colf_p),
                         (c_onesrf[:], ones_rowf_p)]:
            t_cst = S.dma("sp", "cst", dst, src)

        S.op("dve", lambda e: e.tensor_reduce(
            c_rows[0:1, 0:1], c_gqr[:], axis=AX.X, op=ALU.max,
            apply_absolute_value=True), writes=[G["rows"]], waits=tmax(t_cst))
        S.op("dve", lambda e: e.tensor_reduce(
            c_rows[0:1, 1:2], c_gkr[:], axis=AX.X, op=ALU.max,
            apply_absolute_value=True), writes=[G["rows"]], waits=tmax(t_cst))
        S.op("dve", lambda e: e.tensor_tensor(
            c_rows[0:1, 2:3], c_rows[0:1, 0:1], c_rows[0:1, 1:2], ALU.mult),
            writes=[G["rows"]])
        S.op("dve", lambda e: e.tensor_scalar(
            c_rows[0:1, 3:4], c_rows[0:1, 2:3], -SCALE, None, ALU.mult),
            writes=[G["rows"]])
        S.op("pe", lambda e: e.matmul(
            pb[7].ap()[:, 0:1], c_onesrf[:], c_rows[0:1, 3:4],
            start=True, stop=True), reads=[G["rows"]], writes=[pb[7]],
            waits=tmax(t_cst))
        S.op("dve", lambda e: e.tensor_copy(c_negb[:], pb[7].ap()[:, 0:1]),
             reads=[pb[7]], writes=[G["negb"]])

        def bcast_f32(rowap, rowguard, banks, lhsT=None):
            toks = []
            lh = c_onesrf if lhsT is None else lhsT
            for si, bk in enumerate(banks):
                def f(e, si=si, bk=bk, lh=lh):
                    return e.matmul(bk.ap()[:], lh[:],
                                    rowap[0:1, si * 512:si * 512 + 512],
                                    start=True, stop=True)
                toks.append(S.op("pe", f, reads=[rowguard], writes=[bk],
                                 waits=tmax(t_cst)))
            return toks

        a_big = sb("a_big", (128, 32768), BF16)
        a_Y = sb("a_Y", (128, HQ * 1024), BF16)
        h_f32 = a_big.bitcast(F32)
        h_bf = a_Y
        gY = Guard()
        gHb = gY
        gH = Guard()
        gQT = gH
        gKT = gH
        gVT = gH
        if True:
            a_KT = a_big[:, 0:8192]
            a_VT = a_big[:, 8192:16384]
            a_QT = a_big[:, 16384:32768]

            with ExitStack() as pctx:
                def psb(name, shape, dt):
                    return pctx.enter_context(
                        nc.sbuf_tensor(name, list(shape), dt))

                p_xc = psb("p_xc", (128, NCH * 1024), BF16)
                p_cosk = psb("p_cosk", (128, 1024), F32)
                p_sink = psb("p_sink", (128, 1024), F32)
                p_wt = [psb(f"p_wt{i}", (128, NCH * 128), BF16)
                        for i in range(2)]
                p_wv = psb("p_wv", (128, NCH * 512), BF16)
                p_tmp = [psb(f"p_tmp{i}", (128, 1024), F32)
                         for i in range(2)]
                p_rot = [psb(f"p_rot{i}", (128, 1024), F32)
                         for i in range(2)]
                p_ta = [psb(f"p_ta{i}", (128, 1024), F32) for i in range(2)]
                p_sq = [psb(f"p_sq{i}", (128, 1024), BF16)
                        for i in range(2)]
                gxc, gck, gwt0, gwt1, gwv = (Guard() for _ in range(5))
                gtmp = [Guard(), Guard()]
                grot = [Guard(), Guard()]
                gta = [Guard(), Guard()]
                gsq = [Guard(), Guard()]
                gwt = [gwt0, gwt1]

                t_wv = None

                def proj_stage1(dst_bf, dst_g, dst_off, wt_i, cos_ap,
                                sin_ap, cos_g, g_col):
                    par = wt_i
                    tmp, rot, ta, sq = (p_tmp[par], p_rot[par], p_ta[par],
                                        p_sq[par])
                    gT, gR, gA, gQ = gtmp[par], grot[par], gta[par], gsq[par]
                    for si in range(2):
                        bk = [pb[0], pb[1]][si]
                        fns = []
                        for kc in range(NCH):
                            def f(e, si=si, kc=kc, bk=bk, wt_i=wt_i):
                                return e.matmul(
                                    bk.ap()[:],
                                    p_wt[wt_i][:, kc * 128:(kc + 1) * 128],
                                    p_xc[:, kc * 1024 + si * 512:
                                         kc * 1024 + si * 512 + 512],
                                    start=(kc == 0), stop=(kc == NCH - 1))
                            fns.append(f)
                        S.group("pe", fns, reads=[gwt[wt_i], gxc],
                                writes=[bk])
                    for si in range(2):
                        bk = [pb[0], pb[1]][si]
                        def f(e, si=si, bk=bk, tmp=tmp):
                            return e.copy(tmp[:, si * 512:si * 512 + 512],
                                          bk.ap()[:])
                        S.op("act", f, reads=[bk], writes=[gT])
                    S.dma("sp", "shd", rot[0:64, :], tmp[64:128, :],
                          reads=[gT], writes=[gR])
                    S.op("dve", (lambda e, sq=sq, tmp=tmp: e.tensor_tensor(
                        sq[:], tmp[:], tmp[:], ALU.mult)),
                        reads=[gT], writes=[gQ])
                    S.dma("sp", "shd", rot[64:128, :], tmp[0:64, :],
                          reads=[gT], writes=[gR])
                    S.op("dve", (lambda e, tmp=tmp, ta=ta: e.tensor_tensor(
                        ta[:], tmp[:], cos_ap, ALU.mult)),
                        reads=[gT, cos_g], writes=[gA])
                    S.op("dve", (lambda e, rot=rot: e.tensor_tensor(
                        rot[:], rot[:], sin_ap, ALU.mult)),
                        reads=[cos_g], writes=[gR])
                    S.op("dve", (lambda e, tmp=tmp, ta=ta, rot=rot:
                         e.tensor_tensor(tmp[:], ta[:], rot[:], ALU.add)),
                        reads=[gA, gR], writes=[gT])
                    rowb = [pb[2], pb[3]] if par == 0 else [pb[6], pb[7]]
                    rowg = G["rows"] if par == 0 else G["rowsB"]
                    rows_t = c_rows if par == 0 else c_rowsB
                    for si in range(2):
                        bk = rowb[si]
                        def f(e, si=si, bk=bk, sq=sq):
                            return e.matmul(bk.ap()[0:1, :], c_onesc[:],
                                            sq[:, si * 512:si * 512 + 512],
                                            start=True, stop=True)
                        S.op("pe", f, reads=[gQ], writes=[bk],
                             waits=tmax(t_cst))
                    for si in range(2):
                        bk = rowb[si]
                        def f(e, si=si, bk=bk, rows_t=rows_t):
                            return e.activation(
                                rows_t[0:1, si * 512:si * 512 + 512],
                                bk.ap()[0:1, :], AF.Sqrt)
                        S.op("act", f, reads=[bk], writes=[rowg])
                    S.op("dve", (lambda e, rows_t=rows_t: e.tensor_scalar(
                        rows_t[0:1, :], rows_t[0:1, :], 1e-6, None,
                        ALU.add)), writes=[rowg])
                    S.op("dve", (lambda e, rows_t=rows_t: e.reciprocal(
                        rows_t[0:1, :], rows_t[0:1, :])), writes=[rowg])
                    return (dst_bf, dst_g, dst_off, par, g_col)

                def proj_stage2(st):
                    dst_bf, dst_g, dst_off, par, g_col = st
                    tmp = p_tmp[par]
                    gT = gtmp[par]
                    rowg = G["rows"] if par == 0 else G["rowsB"]
                    rows_t = c_rows if par == 0 else c_rowsB
                    bcb = [pb[4], pb[5]]
                    bcast_f32(rows_t, rowg, bcb, lhsT=g_col)
                    for si in range(2):
                        bk = bcb[si]
                        def f(e, si=si, bk=bk, tmp=tmp):
                            return e.tensor_tensor(
                                dst_bf[:, dst_off + si * 512:
                                       dst_off + si * 512 + 512],
                                tmp[:, si * 512:si * 512 + 512],
                                bk.ap()[:], ALU.mult)
                        S.op("dve", f, reads=[gT, bk], writes=[dst_g])

                pend = []

                def proj_head(dst_bf, dst_g, dst_off, wt_i, cos_ap, sin_ap,
                              cos_g, g_col):
                    st = proj_stage1(dst_bf, dst_g, dst_off, wt_i, cos_ap,
                                     sin_ap, cos_g, g_col)
                    pend.append(st)
                    if len(pend) > 1:
                        proj_stage2(pend.pop(0))

                def proj_flush():
                    while pend:
                        proj_stage2(pend.pop(0))


                for ps in range(2):
                    xw = gxc.pre_write()
                    tok2 = None
                    for half in range(2):
                        def fx(e, ps=ps, half=half):
                            return e.dma_start(
                                out=p_xc[:, half * 8192:(half + 1) * 8192],
                                in_=xcp[ps][:, half * 8192:(half + 1) * 8192])
                        tok2 = S.raw("pl", fx, waits=xw if half == 0 else (),
                                     inc="xc", amt=16)
                    gxc.post_write(tok2)
                    if ps == 0:
                        t_wv = S.dma("pl", "wv", p_wv[:], wv_p,
                                     writes=[gwv])
                        t_gw = S.dma("pl", "gw", c_gw[:], gw_p)
                    S.dma("sp", "ck", p_cosk[:], cosk_p[ps], writes=[gck])
                    S.dma("sp", "ck", p_sink[:], sink_p[ps], writes=[gck])
                    for h in range(HK):
                        S.dma("pl", f"wt{h % 2}", p_wt[h % 2][:], wk_p[h],
                              writes=[gwt[h % 2]])
                        proj_head(a_KT, gKT, h * 2048 + ps * 1024, h % 2,
                                  p_cosk[:], p_sink[:], gck, c_gkr)
                    proj_flush()
                    for j in range(8):
                        sc = ps * 8 + j
                        bk = pb[6] if j % 2 == 0 else pb[7]
                        fns = []
                        for kc in range(NCH):
                            def f(e, kc=kc, j=j, bk=bk):
                                return e.matmul(
                                    bk.ap()[:],
                                    p_xc[:, kc * 1024 + j * 128:
                                         kc * 1024 + j * 128 + 128],
                                    p_wv[:, kc * 512:(kc + 1) * 512],
                                    start=(kc == 0), stop=(kc == NCH - 1))
                            fns.append(f)
                        S.group("pe", fns, reads=[gxc, gwv], writes=[bk])
                        def fv(e, sc=sc, bk=bk):
                            return e.copy(a_VT[:, sc * 512:(sc + 1) * 512],
                                          bk.ap()[:])
                        S.op("act", fv, reads=[bk], writes=[gVT])
                    if ps == 1:
                        for h in range(HQ):
                            S.dma("pl", f"wt{h % 2}", p_wt[h % 2][:],
                                  wq_p[h], writes=[gwt[h % 2]])
                            proj_head(a_QT, gQT, h * 1024, h % 2,
                                      p_cosk[:], p_sink[:], gck, c_gqr)

                proj_flush()
                scope_bar = scope_toks(gxc, gck, gwt0, gwt1, gwv,
                                       *gtmp, *grot, *gta, *gsq)

            with ExitStack() as cctx:
                a_mask = cctx.enter_context(
                    nc.sbuf_tensor("a_mask", [128, 12 * 512], BF16))
                a_PT = cctx.enter_context(
                    nc.sbuf_tensor("a_PT", [128, 12 * 512], BF16))
                c_rinvb = [cctx.enter_context(
                    nc.sbuf_tensor(f"c_rinvb{i}", [128, 512], F32))
                    for i in range(2)]
                g_rinvb = [Guard(), Guard()]
                for _g in g_rinvb:
                    _g.seed(scope_bar)
                gmask = Guard()
                gPT = [Guard() for _ in range(12)]
                gmask.seed(scope_bar)
                for _g in gPT:
                    _g.seed(scope_bar)

                for qb in range(2):
                    base = 0 if qb == 0 else 4
                    S.dma("sp", "mk", a_mask[:], masks_p[qb],
                          writes=[gmask])
                    for h in range(HQ):
                        hk = h // 4
                        sbanks = [pb[0], pb[1], pb[6]]
                        for ci in range(12):
                            bk = sbanks[ci % 3]
                            def f(e, ci=ci, bk=bk, h=h, hk=hk, qb=qb,
                                  base=base):
                                return e.matmul(
                                    bk.ap()[:],
                                    a_KT[:, hk * 2048 + (base + ci) * 128:
                                         hk * 2048 + (base + ci) * 128 + 128],
                                    a_QT[:, h * 1024 + qb * 512:
                                         h * 1024 + qb * 512 + 512],
                                    start=True, stop=True)
                            S.op("pe", f, reads=[gKT, gQT], writes=[bk])
                            def fe(e, ci=ci, bk=bk):
                                return e.activation(
                                    a_PT[:, ci * 512:(ci + 1) * 512],
                                    bk.ap()[:], AF.Exp,
                                    bias=c_negb[:, 0:1], scale=SCALE)
                            S.op("act", fe, reads=[bk, G["negb"]],
                                 writes=[gPT[ci]])
                            def fm(e, ci=ci):
                                return e.tensor_tensor(
                                    a_PT[:, ci * 512:(ci + 1) * 512],
                                    a_PT[:, ci * 512:(ci + 1) * 512],
                                    a_mask[:, ci * 512:(ci + 1) * 512],
                                    ALU.mult)
                            S.op("dve", fm, reads=[gmask], writes=[gPT[ci]])
                        fns = []
                        for ci in range(12):
                            def f(e, ci=ci):
                                return e.matmul(
                                    pb[2].ap()[0:1, :], c_onesc[:],
                                    a_PT[:, ci * 512:(ci + 1) * 512],
                                    start=(ci == 0), stop=(ci == 11))
                            fns.append(f)
                        S.group("pe", fns, reads=list(gPT), writes=[pb[2]],
                                waits=tmax(t_cst))
                        fns = []
                        for ci in range(12):
                            def f(e, ci=ci, hk=hk, base=base):
                                scg = base + ci
                                return e.matmul(
                                    pb[4].ap()[:],
                                    a_VT[:, scg * 512 + hk * 128:
                                         scg * 512 + hk * 128 + 128],
                                    a_PT[:, ci * 512:(ci + 1) * 512],
                                    start=(ci == 0), stop=(ci == 11))
                            fns.append(f)
                        S.group("pe", fns, reads=list(gPT) + [gVT],
                                writes=[pb[4]])
                        par = h % 2
                        rows_t = c_rows if par == 0 else c_rowsB
                        rowg = G["rows"] if par == 0 else G["rowsB"]
                        rb = c_rinvb[par]
                        rbg = g_rinvb[par]
                        S.op("dve", (lambda e, rows_t=rows_t: e.reciprocal(
                            rows_t[0:1, 0:512], pb[2].ap()[0:1, :])),
                            reads=[pb[2]], writes=[rowg])
                        bcast_f32(rows_t, rowg, [pb[5]])
                        S.op("dve", (lambda e, rb=rb: e.tensor_copy(
                            rb[:], pb[5].ap()[:])),
                            reads=[pb[5]], writes=[rbg])
                        def fr5(e, h=h, qb=qb, rb=rb):
                            return e.tensor_tensor(
                                a_Y[:, h * 1024 + qb * 512:
                                    h * 1024 + qb * 512 + 512],
                                pb[4].ap()[:], rb[:], ALU.mult)
                        S.op("dve", fr5, reads=[pb[4], rbg],
                             writes=[gY])

                scope_bar = tmax(scope_bar,
                                 scope_toks(gmask, *g_rinvb, *gPT))

            bar_cctx = list(scope_bar)
            with ExitStack() as wctx:
                w_wt = [wctx.enter_context(
                    nc.sbuf_tensor(f"w_wt{i}", [128, NCH * 128], BF16,
                                   side="right"))
                    for i in range(2)]
                w_sq = wctx.enter_context(
                    nc.sbuf_tensor("w_sq", [128, 1024], BF16, side="right"))
                w_t1 = wctx.enter_context(
                    nc.sbuf_tensor("w_t1", [128, 1024], F32, side="right"))
                gwo = [Guard(), Guard()]
                gsqb = Guard()
                gt1 = Guard()
                for _g in gwo + [gsqb, gt1]:
                    _g.seed(scope_bar)

                if debug:
                    S.dma("sp", "dbg", dbg["d_kt"], a_big[:, 0:8192],
                          reads=[gH])
                    S.dma("sp", "dbg", dbg["d_vt"], a_big[:, 8192:16384],
                          reads=[gH])
                    S.dma("sp", "dbg", dbg["d_qt"], a_big[:, 16384:32768],
                          reads=[gH])
                    S.dma("sp", "dbg", dbg["d_y"], a_Y[:], reads=[gY])
                xr_w = gH.pre_write()
                def fxr0(e):
                    return e.dma_start(out=h_f32[:, 0:8192],
                                       in_=xcp[1][:, 0:8192])
                t_xr0 = S.raw("sp", fxr0, waits=xr_w, inc="xr", amt=16)
                def fxrw(e):
                    return e.nop()
                S.raw("sp", fxrw, waits=[("xr", 16)])
                def fxr1(e):
                    return e.dma_start(out=h_f32[:, 8192:16384],
                                       in_=xcp[1][:, 8192:16384])
                t_xr1 = S.raw("sp", fxr1, inc="xr", amt=16)
                gH.wtok = t_xr1
                gH.rtoks = []
                xres_half = [t_xr0, t_xr1]

                for m in range(NCH):
                    S.dma("pl", f"wo{m % 2}", w_wt[m % 2][:], wo_p[m],
                          writes=[gwo[m % 2]])
                    banks = [pb[0], pb[1]] if m % 2 == 0 else [pb[6], pb[7]]
                    for si in range(2):
                        bk = banks[si]
                        fns = []
                        for kc in range(NCH):
                            def f(e, m=m, si=si, kc=kc, bk=bk):
                                return e.matmul(
                                    bk.ap()[:],
                                    w_wt[m % 2][:, kc * 128:(kc + 1) * 128],
                                    a_Y[:, kc * 1024 + si * 512:
                                        kc * 1024 + si * 512 + 512],
                                    start=(kc == 0), stop=(kc == NCH - 1))
                            fns.append(f)
                        S.group("pe", fns, reads=[gwo[m % 2], gY],
                                writes=[bk])
                        def fa(e, m=m, si=si, bk=bk):
                            sl = slice(m * 1024 + si * 512,
                                       m * 1024 + si * 512 + 512)
                            return e.tensor_tensor(h_f32[:, sl], h_f32[:, sl],
                                                   bk.ap()[:], ALU.add)
                        tfa = S.raw("dve", fa, waits=tmax(
                            bk.pre_read(), xres_half[m // 8],
                            ("dve", S.cnt.get("dve", 0))), inc="dve")
                        bk.post_read(tfa)
                        gH.wtok = tfa
                    def fs(e, m=m):
                        return e.tensor_tensor(
                            w_sq[:], h_f32[:, m * 1024:(m + 1) * 1024],
                            h_f32[:, m * 1024:(m + 1) * 1024], ALU.mult)
                    S.op("dve", fs, reads=[gH], writes=[gsqb])
                    stats_matmuls(S, pb, c_onescf, c_onesc, h_f32, gH, w_sq,
                                  gsqb, m, tmax(t_cst))
                    wo_done_pe = S.pe_now()
                ln1 = ln_apply(S, nc, pb, G, c_rows, c_rows2, c_rows3,
                               h_f32, gH, h_bf, gHb, w_t1, gt1, bcast_f32)
                if debug:
                    S.dma("sp", "dbg", dbg["d_h1"], h_f32[:], reads=[gH])

            scope_bar = tmax(scope_bar, scope_toks(*gwo, gsqb, gt1))

        with ExitStack() as fctx:
            def fsb(name, shape, dt):
                return fctx.enter_context(
                    nc.sbuf_tensor(name, list(shape), dt))
            f_scores = fsb("f_scores", (128, 8 * 16), F32)
            f_comb = fsb("f_comb", (128, 8 * 16), F32)
            f_sfc = fsb("f_sfc", (128, 128), F32)
            f_gsc = fsb("f_gsc", (128, 32), F32)
            f_gm = fsb("f_gm", (128, 32), F32)
            f_sm = fsb("f_sm", (128, 16), F32)
            f_tmp = fsb("f_tmp", (128, 128), F32)
            f_eq = fsb("f_eq", (128, 128), F32)
            f_sel = fsb("f_sel", (128, 128), F32)
            f_selm = fsb("f_selm", (128, 128), F32)
            f_colA = fsb("f_colA", (128, 8), F32)
            f_colB = fsb("f_colB", (128, 8), F32)
            f_p1 = fsb("f_p1", (128, 32), F32)
            f_gp = [fsb(f"f_gp{i}", (128, NCH * 128), BF16) for i in range(2)]
            f_up = [fsb(f"f_up{i}", (128, NCH * 128), BF16) for i in range(2)]
            f_dp = [fsb(f"f_dp{i}", (128, NIC * 256), BF16)
                    for i in range(2)]
            f_int = fsb("f_int", (128, NIC * 1024), BF16)
            f_t1 = fsb("f_t1", (128, 512), F32)
            f_t2 = fsb("f_t2", (128, 512), F32)
            f_be = fsb("f_be", (128, 1024), F32)
            f_sqt = fsb("f_sqt", (128, 1024), BF16)
            f_out = [fsb(f"f_out{i}", (128, 512), F32) for i in range(2)]
            c_id = fsb("c_id", (128, 128), F32)
            c_biasE = fsb("c_biasE", (128, 128), F32)
            c_iota = fsb("c_iota", (128, 128), F32)
            c_iotaMB = fsb("c_iotaMB", (128, 128), F32)
            c_ohbig = fsb("c_ohbig", (16, 2048), BF16)
            c_combT = fsb("c_combT", (16, 1024), F32)
            c_combTb = fsb("c_combTb", (16, 1024), BF16)
            G["combT"] = Guard()
            G["combT"].seed(scope_bar)
            t_fc = None
            for dst, src in [(c_id[:], id128_p), (c_biasE[:], biasE_p),
                             (c_iota[:], iotaE_p), (c_iotaMB[:], iotaMB_p),
                             (c_ohbig[:], ohbig_p)]:
                t_fc = S.dma("sp", "fcst", dst, src, waits=bar_cctx)
            ggp = [Guard(), Guard()]
            gup = [Guard(), Guard()]
            gdp = [Guard(), Guard()]
            gint = Guard()
            gt1f = Guard()
            gt2f = Guard()
            gbe = Guard()
            gsq2 = Guard()
            gout = [Guard(), Guard()]
            gcomb = Guard()
            gscores = [Guard() for _ in range(8)]
            for _g in (ggp + gup + gdp):
                _g.seed(bar_cctx)
            for _g in (gout + gscores +
                       [gint, gt1f, gt2f, gbe, gsq2, gcomb]):
                _g.seed(scope_bar)

            router(S, nc, pb, ln1, h_bf, gHb, c_gw, t_gw, c_biasE, c_iota,
                   c_iotaMB, f_scores, gscores, f_comb, gcomb, f_sfc, f_gsc,
                   f_gm, f_sm, f_tmp, f_eq, f_sel, f_selm, f_colA, f_colB,
                   f_p1, tmax(t_gw, t_cst, t_fc, scope_bar))

            def expert_gu(g_src, u_src, scaled):
                icg = 0
                for sl, wdt in enumerate(GP_SLICES):
                    nicl = wdt // 128
                    bi = sl % 2
                    coff = NCH * GP_SLICE_OFF[sl]
                    S.dma("pl", f"gp{bi}", f_gp[bi][:, 0:NCH * wdt],
                          g_src[:, coff:coff + NCH * wdt], writes=[ggp[bi]])
                    S.dma("pl", f"up{bi}", f_up[bi][:, 0:NCH * wdt],
                          u_src[:, coff:coff + NCH * wdt], writes=[gup[bi]])
                    for lic in range(nicl):
                        for tb in range(2):
                            par = (icg * 2 + tb) % 2
                            gb = pb[0] if par == 0 else pb[2]
                            ub = pb[1] if par == 0 else pb[3]
                            fns = []
                            for kc in range(NCH):
                                def fg(e, kc=kc, lic=lic, tb=tb, gb=gb,
                                       bi=bi, wdt=wdt):
                                    return e.matmul(
                                        gb.ap()[:],
                                        f_gp[bi][:, kc * wdt + lic * 128:
                                                 kc * wdt + lic * 128 + 128],
                                        h_bf[:, kc * 1024 + tb * 512:
                                             kc * 1024 + tb * 512 + 512],
                                        start=(kc == 0),
                                        stop=(kc == NCH - 1))
                                fns.append(fg)
                            S.group("pe", fns, reads=[ggp[bi], gHb],
                                    writes=[gb])
                            fns = []
                            for kc in range(NCH):
                                def fu(e, kc=kc, lic=lic, tb=tb, ub=ub,
                                       bi=bi, wdt=wdt):
                                    return e.matmul(
                                        ub.ap()[:],
                                        f_up[bi][:, kc * wdt + lic * 128:
                                                 kc * wdt + lic * 128 + 128],
                                        h_bf[:, kc * 1024 + tb * 512:
                                             kc * 1024 + tb * 512 + 512],
                                        start=(kc == 0),
                                        stop=(kc == NCH - 1))
                                fns.append(fu)
                            S.group("pe", fns, reads=[gup[bi], gHb],
                                    writes=[ub])
                            def f1(e, gb=gb):
                                return e.activation(f_t1[:], gb.ap()[:],
                                                    AF.Silu)
                            S.op("act", f1, reads=[gb], writes=[gt1f])
                            dst = f_int[:, icg * 1024 + tb * 512:
                                        icg * 1024 + tb * 512 + 512]
                            if scaled:
                                def f2(e, ub=ub):
                                    return e.tensor_tensor(
                                        f_t2[:], f_t1[:], ub.ap()[:],
                                        ALU.mult)
                                S.op("dve", f2, reads=[gt1f, ub],
                                     writes=[gt2f])
                                def f3(e, dst=dst, tb=tb):
                                    return e.tensor_tensor(
                                        dst, f_t2[:],
                                        f_be[:, tb * 512:tb * 512 + 512],
                                        ALU.mult)
                                S.op("dve", f3, reads=[gt2f, gbe],
                                     writes=[gint])
                            else:
                                def f3(e, ub=ub, dst=dst):
                                    return e.tensor_tensor(
                                        dst, f_t1[:], ub.ap()[:], ALU.mult)
                                S.op("dve", f3, reads=[gt1f, ub],
                                     writes=[gint])
                        icg += 1

            def do_down(dp_src):
                for dq in range(8):
                    di = dq % 2
                    S.dma("pl", f"dp{di}", f_dp[di][:],
                          dp_src[:, dq * NIC * 256:(dq + 1) * NIC * 256],
                          writes=[gdp[di]])
                    for dmi in range(2):
                        m = dq * 2 + dmi
                        for tb in range(2):
                            bk = pb[4] if (dmi * 2 + tb) % 2 == 0 else pb[5]
                            fns = []
                            for kci in range(NIC):
                                def f(e, kci=kci, dmi=dmi, tb=tb, bk=bk,
                                      di=di):
                                    return e.matmul(
                                        bk.ap()[:],
                                        f_dp[di][:, kci * 256 + dmi * 128:
                                             kci * 256 + dmi * 128 + 128],
                                        f_int[:, kci * 1024 + tb * 512:
                                              kci * 1024 + tb * 512 + 512],
                                        start=(kci == 0),
                                        stop=(kci == NIC - 1))
                                fns.append(f)
                            S.group("pe", fns, reads=[gdp[di], gint],
                                    writes=[bk])
                            def fa(e, m=m, tb=tb, bk=bk):
                                sl = slice(m * 1024 + tb * 512,
                                           m * 1024 + tb * 512 + 512)
                                return e.tensor_tensor(
                                    h_f32[:, sl], h_f32[:, sl], bk.ap()[:],
                                    ALU.add)
                            S.op("dve", fa, reads=[bk], writes=[gH])

            expert_gu(sg_p[0], su_p[0], False)
            do_down(sd_p[0])
            for tc in range(8):
                def ftr(e, tc=tc):
                    return e.transpose(pb[1].ap()[0:16, 0:128],
                                       f_comb[:, tc * 16:(tc + 1) * 16],
                                       c_id[:])
                S.op("pe", ftr, reads=[gcomb], writes=[pb[1]],
                     waits=tmax(t_fc))
                def fcp(e, tc=tc):
                    return e.tensor_copy(
                        c_combT[0:16, tc * 128:(tc + 1) * 128],
                        pb[1].ap()[0:16, 0:128])
                S.op("dve", fcp, reads=[pb[1]], writes=[G["combT"]])
            S.op("dve", lambda e: e.tensor_copy(c_combTb[:], c_combT[:]),
                 reads=[G["combT"]], writes=[G["combT"]])
            for eidx in range(E):
                for si in range(2):
                    bk = pb[6 + si]
                    def f(e, si=si, eidx=eidx, bk=bk):
                        return e.matmul(
                            bk.ap()[:],
                            c_ohbig[:, eidx * 128:(eidx + 1) * 128],
                            c_combTb[0:16, si * 512:si * 512 + 512],
                            start=True, stop=True)
                    S.op("pe", f, reads=[G["combT"]], writes=[bk],
                         waits=tmax(t_fc))
                    def fe(e, si=si, bk=bk):
                        return e.tensor_copy(
                            f_be[:, si * 512:si * 512 + 512], bk.ap()[:])
                    S.op("dve", fe, reads=[bk], writes=[gbe])
                expert_gu(gp_p[eidx], up_p[eidx], True)
                do_down(dp_p[eidx])
            expert_gu(sg_p[1], su_p[1], False)
            do_down(sd_p[1])

            if debug:
                S.dma("sp", "dbg", dbg["d_comb"], f_comb[:], reads=[gcomb])
                S.dma("sp", "dbg", dbg["d_scores"], f_scores[:],
                      reads=list(gscores))
                S.dma("sp", "dbg", dbg["d_hbf"], h_bf[:], reads=[gHb])
                S.dma("sp", "dbg", dbg["d_gw"], c_gw[:], waits=tmax(t_gw))
                S.dma("sp", "dbg", dbg["d_hf"], h_f32[:], reads=[gH])
            for m in range(NCH):
                def fs(e, m=m):
                    return e.tensor_tensor(
                        f_sqt[:], h_f32[:, m * 1024:(m + 1) * 1024],
                        h_f32[:, m * 1024:(m + 1) * 1024], ALU.mult)
                S.op("dve", fs, reads=[gH], writes=[gsq2])
                stats_matmuls(S, pb, c_onescf, c_onesc, h_f32, gH, f_sqt,
                              gsq2, m, tmax(t_cst))
            ln_apply(S, nc, pb, G, c_rows, c_rows2, c_rows3, h_f32, gH,
                     None, None, None, None, bcast_f32,
                     out_bufs=f_out, gout=gout, out_dram=out_d)

        S.raw("sp", lambda e: e.nop(),
              waits=[(sm, S.cnt[sm]) for sm in ("od0", "od1", "dbg")
                     if sm in S.cnt])

        sem_handles = {}
        for name in S.cnt.keys():
            sem_handles[name] = ctx.enter_context(nc.semaphore(name))

        with nc.Block() as blk:
            def run(eng_obj, key):
                for fn, ws, inc, amt in S.ops[key]:
                    for (s, v) in ws:
                        eng_obj.wait_ge(sem_handles[s], v)
                    ins = fn(eng_obj)
                    if inc is not None:
                        ins.then_inc(sem_handles[inc], amt)

            @blk.tensor
            def _(e):
                run(e, "pe")

            @blk.scalar
            def _(e):
                run(e, "act")

            @blk.vector
            def _(e):
                run(e, "dve")

            @blk.sync
            def _(e):
                run(e, "sp")

            @blk.gpsimd
            def _(e):
                run(e, "pl")

    return nc


def stats_matmuls(S, pb, onescf, c_onesc, h_src, gHsrc, sqbuf, gsqb, m,
                  cstw):
    for si in range(2):
        def f1(e, m=m, si=si):
            return e.matmul(pb[2 + si].ap()[0:1, :], onescf[:],
                            h_src[:, m * 1024 + si * 512:
                                  m * 1024 + si * 512 + 512],
                            start=(m == 0), stop=(m == NCH - 1))
        if m == 0:
            tok = S.raw("pe", f1, waits=tmax(gHsrc.pre_read(),
                                             pb[2 + si].pre_write(), cstw),
                        inc="pe")
            pb[2 + si].rtoks = []
        else:
            tok = S.raw("pe", f1, waits=tmax(gHsrc.pre_read()),
                        inc="pe")
        gHsrc.post_read(tok)
        if m == NCH - 1:
            pb[2 + si].post_write(tok)
        def f2(e, m=m, si=si):
            return e.matmul(pb[4 + si].ap()[0:1, :], c_onesc[:],
                            sqbuf[:, si * 512:si * 512 + 512],
                            start=(m == 0), stop=(m == NCH - 1))
        if m == 0:
            tok = S.raw("pe", f2, waits=tmax(gsqb.pre_read(),
                                             pb[4 + si].pre_write(), cstw),
                        inc="pe")
            pb[4 + si].rtoks = []
        else:
            tok = S.raw("pe", f2, waits=tmax(gsqb.pre_read()), inc="pe")
        gsqb.post_read(tok)
        if m == NCH - 1:
            pb[4 + si].post_write(tok)


def ln_apply(S, nc, pb, G, c_rows, c_rows2, c_rows3, h_f32, gH, h_bf, gHb,
             tmpbuf, gt1, bcast_f32, out_bufs=None, gout=None, out_dram=None):
    for si in range(2):
        def f(e, si=si):
            return e.tensor_copy(c_rows[0:1, si * 512:si * 512 + 512],
                                 pb[2 + si].ap()[0:1, :])
        S.op("dve", f, reads=[pb[2 + si]], writes=[G["rows"]])
        def f2(e, si=si):
            return e.tensor_copy(c_rows2[0:1, si * 512:si * 512 + 512],
                                 pb[4 + si].ap()[0:1, :])
        S.op("dve", f2, reads=[pb[4 + si]], writes=[G["rows2"]])
    S.op("dve", lambda e: e.tensor_scalar(
        c_rows[0:1, :], c_rows[0:1, :], 1.0 / 2048.0, None, ALU.mult),
        writes=[G["rows"]])
    S.op("dve", lambda e: e.tensor_scalar(
        c_rows2[0:1, :], c_rows2[0:1, :], 1.0 / 2048.0, None, ALU.mult),
        writes=[G["rows2"]])
    S.op("dve", lambda e: e.tensor_tensor(
        c_rows3[0:1, :], c_rows[0:1, :], c_rows[0:1, :], ALU.mult),
        reads=[G["rows"]], writes=[G["rows3"]])
    S.op("dve", lambda e: e.tensor_tensor(
        c_rows2[0:1, :], c_rows2[0:1, :], c_rows3[0:1, :], ALU.subtract),
        reads=[G["rows3"]], writes=[G["rows2"]])
    S.op("dve", lambda e: e.tensor_scalar(
        c_rows2[0:1, :], c_rows2[0:1, :], 1e-5, None, ALU.add),
        writes=[G["rows2"]])
    S.op("act", lambda e: e.activation(
        c_rows3[0:1, :], c_rows2[0:1, :], AF.Sqrt),
        reads=[G["rows2"]], writes=[G["rows3"]])
    S.op("dve", lambda e: e.reciprocal(c_rows2[0:1, :], c_rows3[0:1, :]),
         reads=[G["rows3"]], writes=[G["rows2"]])
    S.op("dve", lambda e: e.tensor_tensor(
        c_rows3[0:1, :], c_rows[0:1, :], c_rows2[0:1, :], ALU.mult),
        reads=[G["rows"], G["rows2"]], writes=[G["rows3"]])
    rb = bcast_f32(c_rows2, G["rows2"], [pb[2], pb[3]])
    mb = bcast_f32(c_rows3, G["rows3"], [pb[4], pb[5]])
    last = None
    for m in range(NCH):
        otoks = []
        for si in range(2):
            rbk = pb[2 + si]
            mbk = pb[4 + si]
            if out_dram is None:
                tb = tmpbuf[:, 0:512]
                tbg = gt1
                dst = h_f32[:, m * 1024 + si * 512:m * 1024 + si * 512 + 512]
                dstg = gH
            else:
                bi = (m * 2 + si) % 2
                tb = out_bufs[bi][:]
                tbg = gout[bi]
                dst = tb
                dstg = gout[bi]
            def f1(e, m=m, si=si, rbk=rbk, tb=tb):
                return e.tensor_tensor(
                    tb,
                    h_f32[:, m * 1024 + si * 512:m * 1024 + si * 512 + 512],
                    rbk.ap()[:], ALU.mult)
            S.op("dve", f1, reads=[rbk, gH], writes=[tbg])
            def f2(e, dst=dst, tb=tb, mbk=mbk):
                return e.tensor_tensor(dst, tb, mbk.ap()[:], ALU.subtract)
            t2 = S.op("dve", f2, reads=[mbk, tbg], writes=[dstg])
            otoks.append(t2)
            if out_dram is not None:
                bi = (m * 2 + si) % 2
                def fst(e, m=m, si=si, bi=bi):
                    return e.dma_start(
                        out=out_dram[:, m * 1024 + si * 512:
                                     m * 1024 + si * 512 + 512],
                        in_=out_bufs[bi][:])
                last = S.op("sp", fst, reads=[gout[bi]], sem=f"od{bi}",
                            amt=16)
        if out_dram is None:
            def f3(e, m=m):
                return e.copy(h_bf[:, m * 1024:(m + 1) * 1024],
                              h_f32[:, m * 1024:(m + 1) * 1024])
            last = S.op("act", f3, reads=[gH], writes=[gHb])
    return last


def router(S, nc, pb, ln1, h_bf, gHb, c_gw, t_gw, c_biasE, c_iota,
           c_iotaMB, f_scores, gscores, f_comb, gcomb, f_sfc, f_gsc, f_gm,
           f_sm, f_tmp, f_eq, f_sel, f_selm, f_colA, f_colB, f_p1, cstw):
    PAIRS4 = [(0, 1), (0, 2), (0, 3), (1, 2), (1, 3), (2, 3)]
    grt = Guard()
    bk = pb[0]
    for tc in range(8):
        fns = []
        for kc in range(NCH):
            def f(e, kc=kc, tc=tc, bk=bk):
                return e.matmul(
                    bk.ap()[:, tc * 16:(tc + 1) * 16],
                    h_bf[:, kc * 1024 + tc * 128:kc * 1024 + tc * 128 + 128],
                    c_gw[:, kc * 16:(kc + 1) * 16],
                    start=(kc == 0), stop=(kc == NCH - 1))
            fns.append(f)
        S.group("pe", fns, reads=[gHb], writes=[bk], waits=cstw)
    S.op("act", lambda e: e.activation(
        f_scores[:], bk.ap()[:, 0:128], AF.Sigmoid),
        reads=[bk], writes=[gscores[0]])

    def seg(ap, i):
        return ap.rearrange("p (s i) -> p s i", i=i)

    def b0(tensor, nseg, i):
        return bass.AP(tensor.ap().tensor, 0, [[nseg, 128], [1, nseg], [0, i]])

    D = lambda fn, **kw: S.op("dve", fn, writes=[grt], **kw)
    D(lambda e: e.tensor_tensor(f_sfc[:], f_scores[:], c_biasE[:], ALU.add),
      reads=[gscores[0]], waits=cstw)
    first = True
    for (i, j) in PAIRS4:
        D(lambda e, i=i, j=j: e.tensor_tensor(
            f_p1[:], f_sfc[:, i:128:4], f_sfc[:, j:128:4], ALU.add))
        if first:
            D(lambda e: e.tensor_copy(f_gsc[:], f_p1[:]))
            first = False
        else:
            D(lambda e: e.tensor_tensor(f_gsc[:], f_gsc[:], f_p1[:],
                                        ALU.max))
    first = True
    for (i, j) in PAIRS4:
        D(lambda e, i=i, j=j: e.tensor_tensor(
            f_colB[:], f_gsc[:, i:32:4], f_gsc[:, j:32:4], ALU.min))
        if first:
            D(lambda e: e.tensor_copy(f_colA[:], f_colB[:]))
            first = False
        else:
            D(lambda e: e.tensor_tensor(f_colA[:], f_colA[:], f_colB[:],
                                        ALU.max))
    D(lambda e: e.tensor_tensor(seg(f_gm[:], 4), seg(f_gsc[:], 4),
                                b0(f_colA, 8, 4), ALU.is_ge))
    D(lambda e: e.tensor_tensor(seg(f_tmp[:], 4), seg(f_sfc[:], 4),
                                b0(f_gm, 32, 4), ALU.mult))
    D(lambda e: e.memset(f_selm[:], 0.0))
    for it in range(6):
        D(lambda e: e.tensor_reduce(f_colA[:], seg(f_tmp[:], 16), axis=AX.X,
                                    op=ALU.max))
        D(lambda e: e.tensor_tensor(seg(f_eq[:], 16), seg(f_tmp[:], 16),
                                    b0(f_colA, 8, 16), ALU.is_equal))
        D(lambda e: e.tensor_tensor(f_sel[:], f_eq[:], c_iotaMB[:],
                                    ALU.mult))
        D(lambda e: e.tensor_scalar(f_sel[:], f_sel[:], 1.0e4, None,
                                    ALU.add))
        D(lambda e: e.tensor_reduce(f_colA[:], seg(f_sel[:], 16), axis=AX.X,
                                    op=ALU.min))
        D(lambda e: e.tensor_tensor(seg(f_sel[:], 16), c_iota[:].rearrange(
            "p (s i) -> p s i", i=16), b0(f_colA, 8, 16), ALU.is_equal))
        D(lambda e: e.tensor_tensor(f_selm[:], f_selm[:], f_sel[:], ALU.add))
        D(lambda e: e.tensor_scalar(f_sel[:], f_sel[:], 1.0e4, None,
                                    ALU.mult))
        D(lambda e: e.tensor_tensor(f_tmp[:], f_tmp[:], f_sel[:],
                                    ALU.subtract))
    D(lambda e: e.tensor_tensor(f_eq[:], f_scores[:], f_selm[:], ALU.mult),
      reads=[gscores[0]])
    D(lambda e: e.tensor_reduce(f_colA[:], seg(f_eq[:], 16), axis=AX.X,
                                op=ALU.add))
    D(lambda e: e.tensor_scalar(f_colA[:], f_colA[:], 1.0e-20, None,
                                ALU.add))
    D(lambda e: e.reciprocal(f_colA[:], f_colA[:]))
    D(lambda e: e.tensor_scalar(f_colA[:], f_colA[:], 2.5, None, ALU.mult))
    S.op("dve", lambda e: e.tensor_tensor(
        seg(f_comb[:], 16), seg(f_eq[:], 16), b0(f_colA, 8, 16), ALU.mult),
        writes=[gcomb, grt])



def _pack_mblocks(wT, nb):
    a = wT.reshape(NCH, 128, nb, 128).transpose(2, 1, 0, 3)
    return np.ascontiguousarray(a.reshape(nb, 128, NCH * 128))


def _pack_rhs(wT):
    N = wT.shape[1]
    a = wT.reshape(NCH, 128, N).transpose(1, 0, 2)
    return np.ascontiguousarray(a.reshape(128, NCH * N))


def _pack_gu_slices(AT):
    ne, Hh, Ii = AT.shape
    A = AT.reshape(ne, NCH, 128, Ii)
    parts = []
    for w, io in zip(GP_SLICES, GP_SLICE_OFF):
        parts.append(A[:, :, :, io:io + w].transpose(0, 2, 1, 3)
                     .reshape(ne, 128, NCH * w))
    return np.ascontiguousarray(np.concatenate(parts, axis=2))


def _pack_dp(AT):
    ne = AT.shape[0]
    A = AT.reshape(ne, NIC, 128, DM)
    parts = []
    for h in range(8):
        parts.append(A[:, :, :, h * 256:(h + 1) * 256].transpose(0, 2, 1, 3)
                     .reshape(ne, 128, NIC * 256))
    return np.ascontiguousarray(np.concatenate(parts, axis=2))


_PROG_CACHE = {}


def _get_prog():
    if "p" not in _PROG_CACHE:
        _PROG_CACHE["p"] = build_program()
    return _PROG_CACHE["p"]


def kernel(L0_attn_x, L0_attn_wq, L0_attn_wk, L0_attn_wv, L0_attn_wo,
           L0_attn_gamma_q, L0_attn_gamma_k, L0_ffn_gate_weight,
           L0_ffn_e_score_correction_bias, L0_ffn_gate_proj, L0_ffn_up_proj,
           L0_ffn_down_proj, L0_ffn_shared_gate_w, L0_ffn_shared_up_w,
           L0_ffn_shared_down_w):
    x = np.asarray(L0_attn_x, dtype=np.float32)
    nc = _get_prog()

    wq_p = _pack_mblocks(np.ascontiguousarray(
        np.asarray(L0_attn_wq, np.float32).T), 16)
    wk_p = _pack_mblocks(np.ascontiguousarray(
        np.asarray(L0_attn_wk, np.float32).T), 4)
    wv_p = _pack_rhs(np.ascontiguousarray(
        np.asarray(L0_attn_wv, np.float32).T))
    wo_p = _pack_mblocks(np.ascontiguousarray(
        np.asarray(L0_attn_wo, np.float32).T), 16)
    gw_p = _pack_rhs(np.ascontiguousarray(
        np.asarray(L0_ffn_gate_weight, np.float32).T))
    gp_p = _pack_gu_slices(np.asarray(L0_ffn_gate_proj, np.float32)
                           .transpose(0, 2, 1))
    up_p = _pack_gu_slices(np.asarray(L0_ffn_up_proj, np.float32)
                           .transpose(0, 2, 1))
    dp_p = _pack_dp(np.asarray(L0_ffn_down_proj, np.float32)
                    .transpose(0, 2, 1))
    sgT = np.asarray(L0_ffn_shared_gate_w, np.float32).T
    suT = np.asarray(L0_ffn_shared_up_w, np.float32).T
    sdT = np.asarray(L0_ffn_shared_down_w, np.float32).T
    sg_p = _pack_gu_slices(np.stack([sgT[:, :IEXP], sgT[:, IEXP:]], axis=0))
    su_p = _pack_gu_slices(np.stack([suT[:, :IEXP], suT[:, IEXP:]], axis=0))
    sd_p = _pack_dp(np.stack([sdT[:IEXP], sdT[IEXP:]], axis=0))

    inv = (1.0 / 10000.0 ** (np.arange(0, HD, 2, dtype=np.float64) / HD))
    tpos = np.arange(T, dtype=np.float64)
    ang = tpos[:, None] * inv[None, :]
    cosT = np.cos(ang).astype(np.float32)
    sinT = np.sin(ang).astype(np.float32)
    cos_full = np.ascontiguousarray(np.concatenate([cosT, cosT], axis=1).T)
    sin_full = np.ascontiguousarray(np.concatenate([sinT, sinT], axis=1).T)
    sign = np.where(np.arange(128) < 64, -1.0, 1.0).astype(np.float32)
    sinS_full = np.ascontiguousarray(sin_full * sign[:, None])

    ones_col = np.ones((128, 1), dtype=BF)
    ones_rowf = np.ones((1, 128), dtype=np.float32)
    id128 = np.eye(128, dtype=np.float32)
    biasE = np.ascontiguousarray(np.broadcast_to(np.tile(
        np.asarray(L0_ffn_e_score_correction_bias, np.float32), 8),
        (128, 128)))
    iotaE = np.ascontiguousarray(np.broadcast_to(
        np.tile(np.arange(16, dtype=np.float32), 8), (128, 128)))
    iotaMB = np.ascontiguousarray(iotaE - 1.0e4)
    ohbig = np.zeros((16, 2048), dtype=BF)
    for e in range(16):
        ohbig[e, e * 128:(e + 1) * 128] = 1.0

    in_maps = []
    for c in range(8):
        b, r0 = c // 2, (c % 2) * 1024
        other = 1024 - r0
        xT = np.ascontiguousarray(x[b].T)
        xr = xT.reshape(NCH, 128, T)
        xcp = np.ascontiguousarray(np.stack([
            xr[:, :, other:other + 1024].transpose(1, 0, 2)
            .reshape(128, NCH * 1024),
            xr[:, :, r0:r0 + 1024].transpose(1, 0, 2)
            .reshape(128, NCH * 1024)], axis=0))
        cosk = np.ascontiguousarray(np.stack(
            [cos_full[:, other:other + 1024], cos_full[:, r0:r0 + 1024]],
            axis=0))
        sink = np.ascontiguousarray(np.stack(
            [sinS_full[:, other:other + 1024], sinS_full[:, r0:r0 + 1024]],
            axis=0))
        pass_base = [other, r0]
        m = np.zeros((2, 128, 12 * 512), dtype=np.float32)
        parts = np.arange(128)
        for qb in range(2):
            for ci in range(12):
                scg = (0 if qb == 0 else 4) + ci
                pidx = scg // 8
                s_abs = pass_base[pidx] + (scg % 8) * 128 + parts
                q_abs = r0 + qb * 512 + np.arange(512)
                dd = q_abs[None, :] - s_abs[:, None]
                m[qb, :, ci * 512:(ci + 1) * 512] = (
                    (dd >= 0) & (dd <= WINDOW)).astype(np.float32)
        masks = m.astype(BF)

        im = {
            "xcp": xcp, "wq_p": wq_p, "wk_p": wk_p, "wv_p": wv_p,
            "wo_p": wo_p, "gw_p": gw_p, "gp_p": gp_p, "up_p": up_p,
            "dp_p": dp_p, "sg_p": sg_p, "su_p": su_p, "sd_p": sd_p,
            "masks_p": masks,
            "cosk_p": cosk, "sink_p": sink,
            "gq_col": np.ascontiguousarray(
                np.asarray(L0_attn_gamma_q, np.float32).reshape(128, 1)),
            "gk_col": np.ascontiguousarray(
                np.asarray(L0_attn_gamma_k, np.float32).reshape(128, 1)),
            "gq_row": np.ascontiguousarray(
                np.asarray(L0_attn_gamma_q, np.float32).reshape(1, 128)),
            "gk_row": np.ascontiguousarray(
                np.asarray(L0_attn_gamma_k, np.float32).reshape(1, 128)),
            "biasE_p": biasE, "iotaE_p": iotaE, "iotaMB_p": iotaMB,
            "ohbig_p": ohbig, "id128_p": id128,
            "ones_col_p": ones_col, "ones_rowf_p": ones_rowf,
            "ones_colf_p": np.ones((128, 1), dtype=np.float32),
        }
        in_maps.append(im)

    res = run_bass_kernel_spmd(nc, in_maps, core_ids=list(range(8)))
    out = np.empty((B, T, DM), dtype=np.float32)
    for c in range(8):
        b, r0 = c // 2, (c % 2) * 1024
        O = np.asarray(res.results[c]["out"])
        out[b, r0:r0 + 1024, :] = (O.reshape(128, NCH, 1024)
                                   .transpose(1, 0, 2).reshape(DM, 1024).T)
    return out
